# version 13
# baseline (speedup 1.0000x reference)
import numpy as np
import ml_dtypes
import concourse.bass as bass
import concourse.mybir as mybir
from concourse.bass_utils import run_bass_kernel_spmd
from contextlib import ExitStack

F32 = mybir.dt.float32
BF16 = mybir.dt.bfloat16
AF = mybir.ActivationFunctionType
ALU = mybir.AluOpType
AX = mybir.AxisListType

D = 1024
DC = 8
K = 64
T = 2048
NT = 16
NTQ = 4
NE = 8
FF = 4096
NG = 4
FCG = 8
N_CORES = 8
EPS_LN = 1e-5
EPS_DEN = 1e-6

DEBUG = False


def build_program():
    nc = bass.Bass("TRN2", target_bir_lowering=False, debug=False)

    x_d = nc.dram_tensor("x", [T, D], F32, kind="ExternalInput").ap()
    wqt_d = nc.dram_tensor("wqt", [D, K], F32, kind="ExternalInput").ap()
    wkt_d = nc.dram_tensor("wkt", [D, K], F32, kind="ExternalInput").ap()
    wvt_d = nc.dram_tensor("wvt", [D, K], F32, kind="ExternalInput").ap()
    wot_d = nc.dram_tensor("wot", [K, D], F32, kind="ExternalInput").ap()
    rw_d = nc.dram_tensor("rw", [D, NE], F32, kind="ExternalInput").ap()
    rbbc_d = nc.dram_tensor("rbbc", [128, NE], F32, kind="ExternalInput").ap()
    w1_d = nc.dram_tensor("w1", [NE, D, FF], F32, kind="ExternalInput").ap()
    b1r_d = nc.dram_tensor("b1r", [NE, 128, 32], F32, kind="ExternalInput").ap()
    w2_d = nc.dram_tensor("w2", [NE, FF, D], F32, kind="ExternalInput").ap()
    b2_d = nc.dram_tensor("b2", [NE, D], F32, kind="ExternalInput").ap()
    ident_d = nc.dram_tensor("ident", [128, 128], BF16, kind="ExternalInput").ap()
    gsel_d = nc.dram_tensor("gsel", [NE, NE * 128], BF16, kind="ExternalInput").ap()
    masku_d = nc.dram_tensor("masku", [128, 128], F32, kind="ExternalInput").ap()
    pmp_d = nc.dram_tensor("pmp", [128, NE], F32, kind="ExternalInput").ap()
    xp_d = nc.dram_tensor("xp", [7 * T, D], F32, kind="ExternalInput").ap()
    out_d = nc.dram_tensor("out", [T, D], F32, kind="ExternalOutput").ap()
    if DEBUG:
        dbg_q = nc.dram_tensor("dbg_q", [K, T], F32, kind="ExternalOutput").ap()
        dbg_k = nc.dram_tensor("dbg_k", [K, T], F32, kind="ExternalOutput").ap()
        dbg_sini = nc.dram_tensor("dbg_sini", [K, K + 1], F32, kind="ExternalOutput").ap()
        dbg_resa = nc.dram_tensor("dbg_resa", [T, D], F32, kind="ExternalOutput").ap()
        dbg_gates = nc.dram_tensor("dbg_gates", [128, NT * NE], F32, kind="ExternalOutput").ap()
        dbg_xfm = nc.dram_tensor("dbg_xfm", [128, DC * T], F32, kind="ExternalOutput").ap()
        dbg_res0 = nc.dram_tensor("dbg_res0", [T, D], F32, kind="ExternalOutput").ap()
        dbg_stat = nc.dram_tensor("dbg_stat", [128, 16], F32, kind="ExternalOutput").ap()
        dbg_xh = nc.dram_tensor("dbg_xh", [128, D], F32, kind="ExternalOutput").ap()
        dbg_sl = nc.dram_tensor("dbg_sl", [K, K + 1], F32, kind="ExternalOutput").ap()
        dbg_vtm = nc.dram_tensor("dbg_vtm", [128, NT * (K + 1)], F32, kind="ExternalOutput").ap()
        dbg_ktm = nc.dram_tensor("dbg_ktm", [128, NT * K], F32, kind="ExternalOutput").ap()

    ctx = ExitStack()
    E_ = ctx.enter_context
    res = E_(nc.sbuf_tensor("res", [128, NT * D], F32))
    xfm = E_(nc.sbuf_tensor("xfm", [128, DC * T], BF16))
    h = E_(nc.sbuf_tensor("h", [128, FCG * T], BF16))
    w1b = E_(nc.sbuf_tensor("w1b", [128, DC * 1024], BF16))
    w2b = E_(nc.sbuf_tensor("w2b", [128, FCG * 1024], BF16))
    gbc = E_(nc.sbuf_tensor("gbc", [128, T], BF16))
    hscr = [E_(nc.sbuf_tensor(f"hscr{i}", [128, 512], BF16)) for i in range(3)]
    qfm = E_(nc.sbuf_tensor("qfm", [K, T], BF16))
    kfm = E_(nc.sbuf_tensor("kfm", [K, T], BF16))
    ktm = E_(nc.sbuf_tensor("ktm", [128, NT * K], BF16))
    vtm = E_(nc.sbuf_tensor("vtm", [128, NT * (K + 1)], BF16))
    wqT = E_(nc.sbuf_tensor("wqT", [128, DC * K], BF16))
    wkT = E_(nc.sbuf_tensor("wkT", [128, DC * K], BF16))
    wvT = E_(nc.sbuf_tensor("wvT", [128, DC * K], BF16))
    woT = E_(nc.sbuf_tensor("woT", [K, D], BF16))
    rw_s = E_(nc.sbuf_tensor("rw_s", [128, DC * NE], BF16))
    rbbc = E_(nc.sbuf_tensor("rbbc_s", [128, NE], F32))
    b1s = E_(nc.sbuf_tensor("b1s", [128, NE * 32], F32))
    b2s = E_(nc.sbuf_tensor("b2s", [NE, D], BF16))
    ident = E_(nc.sbuf_tensor("ident_s", [128, 128], BF16))
    gsel = E_(nc.sbuf_tensor("gsel_s", [NE, NE * 128], BF16))
    masku = E_(nc.sbuf_tensor("masku_s", [128, 128], F32))
    pmp = E_(nc.sbuf_tensor("pmp_s", [128, NE], F32))
    gtT = E_(nc.sbuf_tensor("gtT", [NE, T], BF16))
    scr = E_(nc.sbuf_tensor("scr", [128, 16], F32))
    sAll = E_(nc.sbuf_tensor("sAll", [K, NT * (K + 1)], BF16))
    xh2 = [E_(nc.sbuf_tensor(f"xh{i}", [128, D], BF16)) for i in range(2)]
    eluA = E_(nc.sbuf_tensor("eluA", [K, 512], F32))
    eluB = E_(nc.sbuf_tensor("eluB", [K, 512], F32))
    bm = [E_(nc.sbuf_tensor(f"bm{i}", [128, 128], BF16)) for i in range(2)]
    otm = [E_(nc.sbuf_tensor(f"otm{i}", [128, K], BF16)) for i in range(2)]
    ofm = [E_(nc.sbuf_tensor(f"ofm{i}", [K, 128], BF16)) for i in range(2)]
    sA = E_(nc.sbuf_tensor("sA", [K, K + 1], BF16))
    kprev = [E_(nc.sbuf_tensor(f"kprev{i}", [128, K], BF16)) for i in range(2)]
    vprev = [E_(nc.sbuf_tensor(f"vprev{i}", [128, K + 1], BF16)) for i in range(2)]
    elp = E_(nc.sbuf_tensor("elp", [128, 2 * K], F32))
    stat = E_(nc.sbuf_tensor("stat", [128, 64], F32))
    mr = E_(nc.sbuf_tensor("mr", [128, 2 * NT], F32))
    lg = E_(nc.sbuf_tensor("lg", [128, NE], F32))
    mx8 = E_(nc.sbuf_tensor("mx8", [128, NE], F32))
    e8 = E_(nc.sbuf_tensor("e8", [128, NE], F32))
    sel8 = E_(nc.sbuf_tensor("sel8", [128, NE], F32))
    gu8 = E_(nc.sbuf_tensor("gu8", [128, NE], F32))
    gts2 = [E_(nc.sbuf_tensor(f"gts{i}", [128, NE], BF16)) for i in range(2)]

    PS = [E_(nc.psum_tensor(f"PS{i}", [128, 512], F32)) for i in range(8)]
    PSB = [p.ap().bitcast(BF16) for p in PS]

    scrF32 = h.ap().bitcast(F32)[:, 7 * 1024: 8 * 1024]
    SEM = {}
    for k in ("t", "v", "s", "g", "y", "cc"):
        SEM[k] = E_(nc.semaphore("sem_" + k))
    block = E_(nc.Block())

    Q = {k: [] for k in ("t", "v", "s", "g", "y")}
    C = {k: 0 for k in ("t", "v", "s", "g", "y", "cc")}
    WAITED = {k: {} for k in ("t", "v", "s", "g", "y")}

    def op(eng, f, inc=0, key=None):
        if inc:
            key = key or eng
            C[key] += inc
            def g_(E, f=f, key=key, inc=inc):
                f(E).then_inc(SEM[key], inc)
            Q[eng].append(g_)
            return C[key]
        Q[eng].append(f)
        return None

    def wt(eng, key, val):
        if not val:
            return
        assert val <= C[key], f"forward wait {eng} on {key}: {val} > {C[key]}"
        if WAITED[eng].get(key, -1) >= val:
            return
        WAITED[eng][key] = val
        Q[eng].append(lambda E, key=key, val=val: E.wait_ge(SEM[key], val))

    n_x = op("y", lambda E: E.dma_start(
        out=res[:, :].rearrange("p (tt d) -> p tt d", tt=NT),
        in_=x_d.rearrange("(tt p) d -> p tt d", p=128)), inc=16)

    def dma_w1(e, g):
        return op("g", lambda E, e=e, g=g: E.dma_start(
            out=w1b[:, :].rearrange("p (c f) -> p c f", c=DC),
            in_=w1_d[e, :, g * 1024:(g + 1) * 1024].rearrange("(c p) f -> p c f", p=128)), inc=16)

    def dma_w2(e, g):
        return op("g", lambda E, e=e, g=g: E.dma_start(
            out=w2b[:, :].rearrange("p (c d) -> p c d", c=FCG),
            in_=w2_d[e, g * 1024:(g + 1) * 1024, :].rearrange("(c p) d -> p c d", p=128)), inc=16)

    n_w1 = dma_w1(0, 0)
    n_w2 = dma_w2(0, 0)
    op("g", lambda E: E.dma_start(out=wqT[:, :].rearrange("p (c k) -> p c k", c=DC),
                                  in_=wqt_d.rearrange("(c p) k -> p c k", p=128)), inc=16)
    op("g", lambda E: E.dma_start(out=wkT[:, :].rearrange("p (c k) -> p c k", c=DC),
                                  in_=wkt_d.rearrange("(c p) k -> p c k", p=128)), inc=16)
    op("g", lambda E: E.dma_start(out=wvT[:, :].rearrange("p (c k) -> p c k", c=DC),
                                  in_=wvt_d.rearrange("(c p) k -> p c k", p=128)), inc=16)
    op("g", lambda E: E.dma_start(out=woT[:, :], in_=wot_d), inc=16)
    op("g", lambda E: E.dma_start(out=rw_s[:, :].rearrange("p (c r) -> p c r", c=DC),
                                  in_=rw_d.rearrange("(c p) r -> p c r", p=128)), inc=16)
    op("g", lambda E: E.dma_start(out=rbbc[:, :], in_=rbbc_d), inc=16)
    op("g", lambda E: E.dma_start(out=b1s[:, :].rearrange("p (e f) -> p e f", e=NE),
                                  in_=b1r_d.rearrange("e p f -> p e f")), inc=16)
    op("g", lambda E: E.dma_start(out=b2s[:, :], in_=b2_d), inc=16)
    op("g", lambda E: E.dma_start(out=ident[:, :], in_=ident_d), inc=16)
    op("g", lambda E: E.dma_start(out=gsel[:, :], in_=gsel_d), inc=16)
    op("g", lambda E: E.dma_start(out=masku[:, :], in_=masku_d), inc=16)
    n_init_dma = op("g", lambda E: E.dma_start(out=pmp[:, :], in_=pmp_d), inc=16)

    op("v", lambda E: E.memset(vtm[:, :], 1.0))

    wt("t", "g", n_init_dma)
    for b in range(8):
        op("t", lambda E, b=b: E.transpose(PSB[b][:, 0:128], ident[:, :],
                                           ident[:, :]), inc=1)
    n_warm = C["t"]
    wt("v", "t", n_warm)
    nwv = op("v", lambda E: E.tensor_copy(stat[:, 8:9], PSB[0][0:128, 0:1]), inc=1)
    wt("t", "v", nwv)
    wt("s", "t", n_warm)

    tr_copy_done = {}
    state = dict(prev_nxh=0, prev_tr_t=0, nex_prev=0)

    lnst = {}
    xh_done = {}

    def ln_stats(i, src=None, war=None):
        p = i % 4
        b = 16 + p * 12
        sl = src if src is not None else res[:, (i % NT) * D:((i % NT) + 1) * D]
        wt("v", "y", n_x)
        wt("s", "y", n_x)
        wt("s", "v", war if war is not None else xh_done.get(i - 4, 0))
        op("s", lambda E, sl=sl, b=b: E.activation(scrF32, sl, AF.Square,
                                                   accum_out=stat[:, b + 8:b + 9]))
        op("s", lambda E, sl=sl, b=b: E.activation(scrF32, sl, AF.Identity,
                                                   accum_out=stat[:, b + 0:b + 1]))
        nsq = op("s", lambda E, b=b: E.copy(stat[:, b + 6:b + 7], stat[:, b + 8:b + 9]), inc=1)
        wt("v", "s", nsq)
        op("v", lambda E, b=b: E.tensor_scalar(stat[:, b + 2:b + 3], stat[:, b + 0:b + 1],
                                               1.0 / D, None, ALU.mult))
        ng1 = op("v", lambda E, b=b: E.tensor_scalar(
            stat[:, b + 3:b + 4], stat[:, b + 8:b + 9], 1.0 / D, EPS_LN,
            ALU.mult, ALU.add), inc=1)
        wt("v", "v", ng1)
        op("v", lambda E, b=b: E.scalar_tensor_tensor(
            stat[:, b + 4:b + 5], stat[:, b + 2:b + 3], 1.0,
            stat[:, b + 2:b + 3], ALU.mult, ALU.mult), inc=1)
        ng2 = C["v"]
        wt("v", "v", ng2)
        nv = op("v", lambda E, b=b: E.tensor_tensor(
            stat[:, b + 5:b + 6], stat[:, b + 3:b + 4], stat[:, b + 4:b + 5],
            ALU.subtract), inc=1)
        wt("s", "v", nv)
        nln = op("s", lambda E, b=b: E.activation(stat[:, b + 6:b + 7],
                                                  stat[:, b + 5:b + 6], AF.Ln), inc=1)
        wt("s", "s", nln)
        op("s", lambda E, b=b: E.activation(stat[:, b + 7:b + 8], stat[:, b + 6:b + 7],
                                            AF.Exp, scale=-0.5))
        ns = op("s", lambda E, b=b: E.copy(stat[:, b + 6:b + 7], stat[:, b + 7:b + 8]), inc=1)
        lnst[p] = ns

    def ln_xhat(i, src=None, xhbuf=None, use_mr=False, mr_rdy=0):
        p = i % 4
        b = 16 + p * 12
        sl = src if src is not None else res[:, (i % NT) * D:((i % NT) + 1) * D]
        dst = xhbuf if xhbuf is not None else xh2[i % 2]
        if use_mr:
            wt("v", "v", mr_rdy)
            mcol = mr[:, 2 * i:2 * i + 1]
            rcol = mr[:, 2 * i + 1:2 * i + 2]
        else:
            wt("v", "s", lnst[p])
            mcol = stat[:, b + 2:b + 3]
            rcol = stat[:, b + 7:b + 8]
        if xhbuf is None:
            wt("v", "t", state.get(("tr_t", i % 2), 0))
        else:
            wt("v", "t", state.get(("tr_tp", i % 4), 0))
        op("v", lambda E, sl=sl, dst=dst, mcol=mcol, rcol=rcol: E.tensor_scalar(
            dst[:, :] if hasattr(dst, "ap") else dst, sl, mcol,
            rcol, ALU.subtract, ALU.mult))
        nxh = op("v", lambda E: E.tensor_copy(stat[:, 8:9], stat[:, 12:13]), inc=1)
        xh_done[i] = nxh
        return nxh

    def emit_transposes(i, nxh, banks=(3, 4), xhbuf=None, pk=None):
        wt("t", "v", nxh)
        wt("t", "g", n_init_dma)
        src = xhbuf if xhbuf is not None else xh2[i % 2]
        last_t = 0
        for cp in range(DC // 2):
            b = banks[cp % 2]
            key = ("tr", b)
            wt("t", "v", tr_copy_done.get(key, 0))
            for half in range(2):
                c = cp * 2 + half
                op("t", lambda E, c=c, b=b, half=half, src=src: E.transpose(
                    PSB[b][:, half * 128:(half + 1) * 128],
                    (src[:, :] if hasattr(src, "ap") else src)[:, c * 128:(c + 1) * 128],
                    ident[:, :]),
                    inc=(1 if half == 1 else 0))
            ntr = C["t"]
            wt("v", "t", ntr)
            c0 = cp * 2
            ncp = op("v", lambda E, c0=c0, b=b, i=i: E.tensor_copy(
                xfm[:, :].rearrange("p (c t) -> p c t", c=DC)[:, c0:c0 + 2,
                                                             i * 128:(i + 1) * 128],
                PSB[b][:, 0:256].rearrange("p (c t) -> p c t", c=2)), inc=1)
            tr_copy_done[key] = ncp
            last_t = ntr
        if pk is not None:
            state[("tr_tp", pk)] = last_t
        else:
            state[("tr_t", i % 2)] = last_t
        return C["v"]

    kv_ready = [0] * NT
    ktr_copy = [0] * NT
    qk_read = {}
    nfm_last = 0

    ln_stats(0)
    for i in range(NT):
        if i + 1 < NT:
            ln_stats(i + 1)
        nxh = ln_xhat(i)
        if DEBUG and i == 0:
            wt("g", "v", nxh)
            op("g", lambda E: E.dma_start(out=dbg_stat, in_=stat[:, :]), inc=16)
            op("g", lambda E: E.dma_start(out=dbg_xh, in_=xh2[0][:, :]), inc=16)
            op("y", lambda E: E.dma_start(
                out=dbg_res0.rearrange("(tt p) d -> p tt d", p=128),
                in_=res[:, :].rearrange("p (tt d) -> p tt d", tt=NT)), inc=16, key="y")
        n_xfm_i = emit_transposes(i, nxh)

        wt("t", "g", n_init_dma)
        wt("t", "v", n_xfm_i)
        wt("t", "v", qk_read.get("vps", 0))
        for c in range(DC):
            op("t", lambda E, c=c, i=i: E.matmul(
                PS[2][0:128, 0:K],
                xfm[:, c * T + i * 128: c * T + i * 128 + 128],
                wvT[:, c * K:(c + 1) * K],
                start=(c == 0), stop=(c == DC - 1)),
                inc=(1 if c == DC - 1 else 0))
        n_vmm = C["t"]
        wt("v", "t", n_vmm)
        nvc = op("v", lambda E, i=i: E.tensor_copy(
            vtm[:, i * (K + 1): i * (K + 1) + K], PS[2][0:128, 0:K]), inc=1)
        qk_read["vps"] = nvc

        if i % 4 == 3:
            tq = i // 4
            for (psb, wT, dstfm) in ((0, wqT, qfm), (1, wkT, kfm)):
                keyq = ("qkps", psb)
                wt("t", "v", n_xfm_i)
                wt("t", "v", qk_read.get(keyq, 0))
                for c in range(DC):
                    op("t", lambda E, c=c, tq=tq, psb=psb, wT=wT: E.matmul(
                        PS[psb][0:K, :],
                        wT[:, c * K:(c + 1) * K],
                        xfm[:, c * T + tq * 512: c * T + (tq + 1) * 512],
                        start=(c == 0), stop=(c == DC - 1)),
                        inc=(1 if c == DC - 1 else 0))
                nqk = C["t"]
                wt("v", "t", nqk)
                op("v", lambda E, psb=psb: E.tensor_scalar(
                    eluA[:, :], PS[psb][0:K, :], 0.0, None, ALU.min))
                nrd = op("v", lambda E, psb=psb: E.tensor_scalar(
                    eluB[:, :], PS[psb][0:K, :], 0.0, None, ALU.max), inc=1)
                qk_read[keyq] = nrd
                wt("s", "v", nrd)
                nex = op("s", lambda E: E.activation(eluA[:, :], eluA[:, :], AF.Exp), inc=1)
                wt("v", "s", nex)
                wt("v", "v", nrd)
                nfm_last = op("v", lambda E, dstfm=dstfm, tq=tq: E.tensor_tensor(
                    dstfm[:, tq * 512:(tq + 1) * 512], eluA[:, :], eluB[:, :], ALU.add), inc=1)
            for ii in range(tq * 4, tq * 4 + 4):
                wt("t", "v", nfm_last)
                wt("t", "v", ktr_copy[ii - 1] if ii > 0 else 0)
                nkt = op("t", lambda E, ii=ii: E.transpose(
                    PSB[5][0:128, 0:K], kfm[:, ii * 128:(ii + 1) * 128],
                    ident[0:K, 0:K]), inc=1)
                wt("v", "t", nkt)
                nkc = op("v", lambda E, ii=ii: E.tensor_copy(
                    ktm[:, ii * K:(ii + 1) * K], PSB[5][0:128, 0:K]), inc=1)
                ktr_copy[ii] = nkc
                kv_ready[ii] = nkc
    n_sloc = C["t"]

    if DEBUG:
        wt("g", "v", C["v"])
        wt("g", "s", C["s"])
        op("g", lambda E: E.dma_start(out=dbg_xfm, in_=xfm[:, :]), inc=16)
        op("g", lambda E: E.dma_start(out=dbg_q, in_=qfm[:, :]), inc=16)
        op("g", lambda E: E.dma_start(out=dbg_k, in_=kfm[:, :]), inc=16)
        op("g", lambda E: E.dma_start(out=dbg_vtm, in_=vtm[:, :]), inc=16)
        op("g", lambda E: E.dma_start(out=dbg_ktm, in_=ktm[:, :]), inc=16)
    hF = h.ap().bitcast(F32)
    stg_s = [0, 0, 0, 0]
    stg_v = [0, 0, 0, 0]
    kv_ps0 = 0
    kv_ps2 = 0
    smm_done = [0, 0]
    kv_done = [0, 0]
    n_xp = [0] * 112

    def repl_stageA(k):
        r = k % 4
        stage = hF[:, 1024 + r * 1024: 2048 + r * 1024]
        wt("y", "s", stg_s[r])
        wt("y", "v", stg_v[r])
        n_xp[k] = op("y", lambda E, k=k, stage=stage: E.dma_start(
            out=stage, in_=xp_d[k * 128:(k + 1) * 128, :]), inc=16, key="y")
        ti = NT + k
        wt("v", "y", n_xp[k])
        wt("s", "y", n_xp[k])
        ln_stats(ti, src=stage)
        stg_s[r] = C["s"]

    def repl_stageB(k):
        r = k % 4
        stage = hF[:, 1024 + r * 1024: 2048 + r * 1024]
        xhp = h.ap()[:, 10240 + r * 1024: 10240 + (r + 1) * 1024]
        ti = NT + k
        nxh = ln_xhat(ti, src=stage, xhbuf=xhp)
        stg_v[r] = nxh
        n_xfm_k = emit_transposes(ti % 2, nxh, xhbuf=xhp, pk=r)
        return n_xfm_k

    def repl_stageC(k, n_xfm_k):
        nonlocal kv_ps0, kv_ps2
        ti = NT + k
        slot = k // NT
        wt("t", "v", n_xfm_k)
        wt("t", "v", kv_ps0)
        for c in range(DC):
            op("t", lambda E, c=c, ti=ti: E.matmul(
                PS[0][:, 0:K],
                xfm[:, c * T + (ti % 2) * 128: c * T + (ti % 2) * 128 + 128],
                wkT[:, c * K:(c + 1) * K],
                start=(c == 0), stop=(c == DC - 1)), inc=(1 if c == DC - 1 else 0))
        nk = C["t"]
        wt("t", "v", kv_ps2)
        for c in range(DC):
            op("t", lambda E, c=c, ti=ti: E.matmul(
                PS[2][:, 0:K],
                xfm[:, c * T + (ti % 2) * 128: c * T + (ti % 2) * 128 + 128],
                wvT[:, c * K:(c + 1) * K],
                start=(c == 0), stop=(c == DC - 1)), inc=(1 if c == DC - 1 else 0))
        nv_ = C["t"]
        rot = k % 2
        wt("v", "t", nk)
        wt("v", "t", smm_done[rot])
        op("v", lambda E: E.tensor_scalar(elp[:, 0:K], PS[0][:, 0:K], 0.0, None, ALU.min))
        nrd = op("v", lambda E: E.tensor_scalar(elp[:, K:2 * K], PS[0][:, 0:K], 0.0,
                                                None, ALU.max), inc=1)
        kv_ps0 = nrd
        wt("s", "v", nrd)
        nex = op("s", lambda E: E.activation(elp[:, 0:K], elp[:, 0:K], AF.Exp), inc=1)
        wt("v", "s", nex)
        op("v", lambda E, rot=rot: E.tensor_tensor(kprev[rot][:, :], elp[:, 0:K],
                                                   elp[:, K:2 * K], ALU.add))
        wt("v", "t", nv_)
        op("v", lambda E, rot=rot, slot=slot: E.tensor_scalar(
            vprev[rot][:, 0:K], PS[2][:, 0:K], pmp[:, slot:slot + 1], None, ALU.mult))
        nkv = op("v", lambda E, rot=rot, slot=slot: E.tensor_scalar(
            vprev[rot][:, K:K + 1], pmp[:, slot:slot + 1], 1.0, None, ALU.mult), inc=1)
        kv_ps2 = nkv
        wt("t", "v", nkv)
        nsm = op("t", lambda E, rot=rot, k=k: E.matmul(
            PS[6][0:K, 0:K + 1], kprev[rot][:, :], vprev[rot][:, :],
            start=(k == 0), stop=False), inc=1)
        smm_done[rot] = nsm

    repl_stageA(0)
    repl_stageA(1)
    pend_B = None
    for k in range(112):
        if k + 2 < 112:
            repl_stageA(k + 2)
        nxf = repl_stageB(k)
        if pend_B is not None:
            repl_stageC(*pend_B)
        pend_B = (k, nxf)
    repl_stageC(*pend_B)
    n_repl_s = C["t"]

    numI = h
    bm_done = [0, 0]
    bm_read = [0, 0]
    numused = [0, 0]
    scorein = [0, 0]
    wt("t", "v", C["v"])
    for j in range(NT):
        sb = j % 2
        wt("t", "v", scorein[sb])
        nB = op("t", lambda E, j=j, sb=sb: E.matmul(
            PS[sb][:, 0:128], kfm[:, j * 128:(j + 1) * 128],
            qfm[:, j * 128:(j + 1) * 128], start=True, stop=True), inc=1)
        wt("v", "t", nB)
        wt("v", "t", bm_read[sb])
        nbm = op("v", lambda E, sb=sb: E.tensor_tensor(
            bm[sb][:, :], PS[sb][:, 0:128], masku[:, :], ALU.mult), inc=1)
        bm_done[sb] = nbm
        scorein[sb] = nbm
        wt("t", "v", bm_done[sb])
        wt("t", "v", numused[sb])
        nnum = op("t", lambda E, j=j, sb=sb: E.matmul(
            PS[2 + sb][:, 0:K + 1], bm[sb][:, :],
            vtm[:, j * (K + 1):(j + 1) * (K + 1)], start=True, stop=True), inc=1)
        bm_read[sb] = nnum
        wt("v", "t", nnum)
        nev = op("v", lambda E, j=j, sb=sb: E.tensor_copy(
            numI[:, j * (K + 1):(j + 1) * (K + 1)], PS[2 + sb][:, 0:K + 1]), inc=1)
        numused[sb] = nev
    n_intra = C["v"]

    wt("v", "t", n_repl_s)
    ncopy = op("v", lambda E: E.tensor_copy(
        sAll[:, 0:K + 1], PS[6][0:K, 0:K + 1]), inc=1)
    if DEBUG:
        wt("g", "v", ncopy)
        op("g", lambda E: E.dma_start(out=dbg_sini, in_=sAll[:, 0:K + 1]), inc=16)
    for j in range(1, NT):
        wt("t", "v", ncopy)
        nupd = op("t", lambda E, j=j: E.matmul(
            PS[6][0:K, 0:K + 1], ktm[:, (j - 1) * K:j * K],
            vtm[:, (j - 1) * (K + 1):j * (K + 1)], start=False, stop=False), inc=1)
        wt("v", "t", nupd)
        ncopy = op("v", lambda E, j=j: E.tensor_copy(
            sAll[:, j * (K + 1):(j + 1) * (K + 1)], PS[6][0:K, 0:K + 1]), inc=1)
    n_states = ncopy

    numused = [n_intra, n_intra]
    ot_copy = [0, 0]
    ofm_read = [0, 0]
    otm_read = [0, 0]
    ao_add = [0, 0]
    nnum_of = [0] * NT
    res_attn_done = [0] * NT
    wt("t", "v", n_states)

    def emit_num(j):
        sb = j % 2
        wt("t", "v", numused[sb])
        op("t", lambda E, j=j, sb=sb: E.matmul(
            PS[2 + sb][:, 0:K + 1], qfm[:, j * 128:(j + 1) * 128],
            sAll[:, j * (K + 1):(j + 1) * (K + 1)], start=True, stop=False))
        nnum_of[j] = op("t", lambda E, j=j, sb=sb: E.matmul(
            PS[2 + sb][:, 0:K + 1], ident[:, :],
            numI[:, j * (K + 1):(j + 1) * (K + 1)], start=False, stop=True), inc=1)

    def emit_den(j):
        sb = j % 2
        wt("v", "t", nnum_of[j])
        nd1 = op("v", lambda E, sb=sb: E.tensor_scalar(
            stat[:, 9:10], PS[2 + sb][:, K:K + 1], EPS_DEN, None, ALU.add), inc=1)
        wt("v", "v", nd1)
        nd2 = op("v", lambda E: E.reciprocal(stat[:, 10:11], stat[:, 9:10]), inc=1)
        wt("v", "v", nd2)
        wt("v", "t", otm_read[sb])
        not_ = op("v", lambda E, sb=sb: E.tensor_scalar(
            otm[sb][:, :], PS[2 + sb][:, 0:K], stat[:, 10:11], None, ALU.mult), inc=1)
        numused[sb] = not_
        return not_

    def emit_out(j, not_):
        sb = j % 2
        wt("t", "v", not_)
        wt("t", "v", ot_copy[sb])
        ntr = op("t", lambda E, sb=sb: E.transpose(
            PSB[4][0:K, sb * 128:(sb + 1) * 128], otm[sb][:, :], ident[:, :]), inc=1)
        otm_read[sb] = ntr
        wt("v", "t", ntr)
        wt("v", "t", ofm_read[sb])
        ncp = op("v", lambda E, sb=sb: E.tensor_copy(
            ofm[sb][:, :], PSB[4][0:K, sb * 128:(sb + 1) * 128]), inc=1)
        ot_copy[sb] = ncp
        wt("t", "v", ncp)
        for half, bank in ((0, 5), (1, 7)):
            wt("t", "v", ao_add[half])
            nao = op("t", lambda E, sb=sb, half=half, bank=bank: E.matmul(
                PS[bank][:, :], ofm[sb][:, :], woT[:, half * 512:(half + 1) * 512],
                start=True, stop=True), inc=1)
            ofm_read[sb] = nao
            wt("v", "t", nao)
            nadd = op("v", lambda E, j=j, half=half, bank=bank: E.tensor_tensor(
                res[:, j * D + half * 512: j * D + (half + 1) * 512],
                PS[bank][:, :],
                res[:, j * D + half * 512: j * D + (half + 1) * 512], ALU.add), inc=1)
            ao_add[half] = nadd
        res_attn_done[j] = ao_add[1]

    c1_rdy = [0] * NT
    c1_war = [0] * NT

    def c1_prestats(i):
        p = i % 4
        b = 16 + p * 12
        wt("s", "v", res_attn_done[i])
        wt("v", "v", res_attn_done[i])
        ln_stats(i, war=c1_war[i - 4] if i >= 4 else None)
        wt("v", "s", lnst[p])
        op("v", lambda E, i=i, b=b: E.tensor_copy(mr[:, 2 * i:2 * i + 1],
                                                  stat[:, b + 2:b + 3]))
        nmr = op("v", lambda E, i=i, b=b: E.tensor_copy(mr[:, 2 * i + 1:2 * i + 2],
                                                        stat[:, b + 7:b + 8]), inc=1)
        c1_rdy[i] = nmr
        c1_war[i] = nmr

    emit_num(0)
    for j in range(1, NT):
        prev_j = j - 1
        not_prev = emit_den(prev_j)
        emit_num(j)
        emit_out(prev_j, not_prev)
        c1_prestats(prev_j)
    emit_out(NT - 1, emit_den(NT - 1))
    c1_prestats(NT - 1)

    n_dbga = 0
    if DEBUG:
        wt("y", "v", C["v"])
        n_dbga = op("y", lambda E: E.dma_start(
            out=dbg_resa.rearrange("(tt p) d -> p tt d", p=128),
            in_=res[:, :].rearrange("p (tt d) -> p tt d", tt=NT)), inc=16, key="y")

    wt("t", "v", C["v"])
    wt("t", "s", C["s"])
    wt("s", "t", C["t"])
    wt("s", "v", C["v"])
    gts_read = 0
    lg_read = 0
    gt_copy = 0
    gates_done = 0
    pending_gt = None

    def flush_gt():
        nonlocal pending_gt
        if pending_gt is None:
            return
        ii, ngt_p = pending_gt
        pending_gt = None
        nonlocal gts_read, gt_copy, gates_done
        wt("t", "v", ngt_p)
        wt("t", "v", gt_copy)
        ngtr = op("t", lambda E, ii=ii: E.transpose(PSB[5][0:NE, 0:128],
                                                    gts2[ii % 2][:, :],
                                                    ident[:, :]), inc=1)
        gts_read = ngtr
        wt("v", "t", ngtr)
        gt_copy = op("v", lambda E, ii=ii: E.tensor_copy(
            gtT[:, ii * 128:(ii + 1) * 128], PSB[5][0:NE, 0:128]), inc=1)
        gates_done = gt_copy

    for i in range(NT):
        nxh = ln_xhat(i, use_mr=True, mr_rdy=c1_rdy[i])
        n_xfm_i = emit_transposes(i, nxh, banks=(3, 4))
        wt("t", "v", n_xfm_i)
        wt("t", "v", lg_read)
        for c in range(DC):
            op("t", lambda E, c=c, i=i: E.matmul(
                PS[2][:, 0:NE],
                xfm[:, c * T + i * 128: c * T + i * 128 + 128],
                rw_s[:, c * NE:(c + 1) * NE],
                start=(c == 0), stop=(c == DC - 1)),
                inc=(1 if c == DC - 1 else 0))
        nlg = C["t"]
        flush_gt()
        wt("v", "t", nlg)
        wt("v", "s", state["nex_prev"])
        lg_read = op("v", lambda E: E.tensor_tensor(lg[:, :], PS[2][:, 0:NE],
                                                    rbbc[:, :], ALU.add), inc=1)
        wt("v", "v", lg_read)
        nmx = op("v", lambda E: E.max(out=mx8[:, :], in_=lg[:, :]), inc=1)
        wt("v", "v", nmx)
        op("v", lambda E: E.tensor_scalar(stat[:, 11:12], mx8[:, 0:1], -1.0, None, ALU.mult))
        nsl = op("v", lambda E: E.tensor_scalar(sel8[:, :], lg[:, :], mx8[:, 1:2],
                                                None, ALU.is_ge), inc=1)
        wt("s", "v", nsl)
        op("s", lambda E: E.activation(e8[:, :], lg[:, :], AF.Exp,
                                       bias=stat[:, 11:12]))
        nex = op("s", lambda E: E.copy(scr[:, 0:1], e8[:, 0:1]), inc=1)
        state["nex_prev"] = nex
        wt("v", "s", nex)
        wt("v", "v", nsl)
        ngu = op("v", lambda E: E.tensor_tensor(gu8[:, :], e8[:, :], sel8[:, :], ALU.mult), inc=1)
        wt("v", "v", ngu)
        ngs = op("v", lambda E: E.reduce_sum(stat[:, 12:13], gu8[:, :], axis=AX.X), inc=1)
        wt("v", "v", ngs)
        nrc = op("v", lambda E: E.reciprocal(stat[:, 13:14], stat[:, 12:13]), inc=1)
        wt("v", "v", nrc)
        wt("v", "t", gts_read)
        wt("v", "g", state.get("dbg_g", 0))
        ngt = op("v", lambda E, i=i: E.tensor_scalar(gts2[i % 2][:, :], gu8[:, :],
                                                     stat[:, 13:14], None, ALU.mult), inc=1)
        if DEBUG:
            wt("g", "v", ngt)
            state["dbg_g"] = op("g", lambda E, i=i: E.dma_start(
                out=dbg_gates[:, i * NE:(i + 1) * NE], in_=gts2[i % 2][:, :]), inc=16)
        pending_gt = (i, ngt)
    flush_gt()

    wt("t", "v", C["v"])
    wt("t", "s", C["s"])
    if DEBUG:
        wt("v", "y", n_dbga)
    hps_free = {0: 0, 1: 0, 7: 0}
    gm_tq = [0, 0, 0, 0]
    hscr_free = [0, 0, 0, 0]
    out_free = {(0, 0): 0, (0, 1): 0, (1, 0): 0, (1, 1): 0}
    gbc_copy = 0
    last_gm = 0
    w1_ready = n_w1
    w2_ready = n_w2

    for e in range(NE):
        wt("t", "v", gates_done)
        for half in range(4):
            wt("t", "v", gbc_copy)
            ngb = op("t", lambda E, e=e, half=half: E.matmul(
                PS[6][:, :], gsel[:, e * 128:(e + 1) * 128],
                gtT[:, half * 512:(half + 1) * 512], start=True, stop=True), inc=1)
            wt("v", "t", ngb)
            wt("v", "v", last_gm)
            gbc_copy = op("v", lambda E, half=half: E.tensor_copy(
                gbc[:, half * 512:(half + 1) * 512], PS[6][:, :]), inc=1)
        for g in range(NG):
            wt("t", "g", w1_ready)
            kidx = 0
            for fc in range(FCG):
                for tq in range(NTQ):
                    hb = (0, 1, 7)[kidx % 3]
                    wt("t", "s", hps_free[hb])
                    for c in range(DC):
                        op("t", lambda E, c=c, fc=fc, tq=tq, hb=hb: E.matmul(
                            PS[hb][:, :],
                            w1b[:, c * 1024 + fc * 128: c * 1024 + fc * 128 + 128],
                            xfm[:, c * T + tq * 512: c * T + (tq + 1) * 512],
                            start=(c == 0), stop=(c == DC - 1)),
                            inc=(1 if c == DC - 1 else 0))
                    nmm = C["t"]
                    sc = kidx % 3
                    wt("s", "t", nmm)
                    wt("s", "v", hscr_free[sc])
                    ngl = op("s", lambda E, hb=hb, e=e, g=g, fc=fc, sc=sc: E.activation(
                        hscr[sc][:, :], PS[hb][:, :], AF.Gelu,
                        bias=b1s[:, e * 32 + g * FCG + fc: e * 32 + g * FCG + fc + 1]), inc=1)
                    hps_free[hb] = ngl
                    wt("v", "s", ngl)
                    wt("v", "v", gbc_copy)
                    ngm = op("v", lambda E, fc=fc, tq=tq, sc=sc: E.tensor_tensor(
                        h[:, fc * T + tq * 512: fc * T + (tq + 1) * 512],
                        hscr[sc][:, :], gbc[:, tq * 512:(tq + 1) * 512], ALU.mult), inc=1)
                    hscr_free[sc] = ngm
                    if fc == FCG - 1:
                        gm_tq[tq] = ngm
                    kidx += 1
            h_written = C["v"]
            last_gm = h_written
            mm1_last = C["t"]
            nxt = (e, g + 1) if g + 1 < NG else ((e + 1, 0) if e + 1 < NE else None)
            if nxt is not None:
                wt("g", "t", mm1_last)
                w1_ready = dma_w1(*nxt)
            wt("t", "g", w2_ready)
            first = (e == 0 and g == 0)
            for s16 in range(NT):
                wt("t", "v", gm_tq[s16 // 4])
                slot = s16 % 2
                for dh in range(2):
                    bank = 2 + slot * 2 + dh
                    wt("t", "v", out_free[(slot, dh)])
                    if first:
                        op("t", lambda E, s16=s16, dh=dh, bank=bank: E.matmul(
                            PS[bank][:, :], gtT[:, s16 * 128:(s16 + 1) * 128],
                            b2s[:, dh * 512:(dh + 1) * 512], start=True, stop=False))
                    for fc in range(FCG):
                        op("t", lambda E, s16=s16, dh=dh, fc=fc, bank=bank, first=first: E.matmul(
                            PS[bank][:, :],
                            h[:, fc * T + s16 * 128: fc * T + s16 * 128 + 128],
                            w2b[:, fc * 1024 + dh * 512: fc * 1024 + dh * 512 + 512],
                            start=(fc == 0 and not first), stop=(fc == FCG - 1)),
                            inc=(1 if fc == FCG - 1 else 0))
                    nmm2 = C["t"]
                    wt("v", "t", nmm2)
                    nev = op("v", lambda E, s16=s16, dh=dh, bank=bank: E.tensor_tensor(
                        res[:, s16 * D + dh * 512: s16 * D + (dh + 1) * 512],
                        PS[bank][:, :],
                        res[:, s16 * D + dh * 512: s16 * D + (dh + 1) * 512],
                        ALU.add), inc=1)
                    out_free[(slot, dh)] = nev
                if e == NE - 1 and g == NG - 1:
                    wt("y", "v", out_free[(slot, 1)])
                    op("y", lambda E, s16=s16: E.dma_start(
                        out=out_d[s16 * 128:(s16 + 1) * 128, :],
                        in_=res[:, s16 * D:(s16 + 1) * D]), inc=16)
            mm2_last = C["t"]
            if nxt is not None:
                wt("g", "t", mm2_last)
                w2_ready = dma_w2(*nxt)

    wt("y", "y", C["y"])
    if DEBUG:
        wt("g", "g", C["g"])

    @block.gpsimd
    def _(En):
        for f in Q["g"]:
            f(En)

    @block.tensor
    def _(En):
        for f in Q["t"]:
            f(En)

    @block.vector
    def _(En):
        for f in Q["v"]:
            f(En)

    @block.scalar
    def _(En):
        for f in Q["s"]:
            f(En)

    @block.sync
    def _(En):
        for f in Q["y"]:
            f(En)

    ctx.close()
    return nc


_NC_CACHE = None


def _get_nc():
    global _NC_CACHE
    if _NC_CACHE is None:
        _NC_CACHE = build_program()
    return _NC_CACHE


def kernel(L0_attn_x, L0_attn_Wq, L0_attn_Wk, L0_attn_Wv, L0_attn_Wo,
           L0_ffn_router_w, L0_ffn_router_b, L0_ffn_w1, L0_ffn_b1,
           L0_ffn_w2, L0_ffn_b2):
    x = np.ascontiguousarray(np.asarray(L0_attn_x, dtype=np.float32).reshape(16384, 1024))
    wq = np.asarray(L0_attn_Wq, dtype=np.float32)
    wk = np.asarray(L0_attn_Wk, dtype=np.float32)
    wv = np.asarray(L0_attn_Wv, dtype=np.float32)
    wo = np.asarray(L0_attn_Wo, dtype=np.float32)
    rw = np.ascontiguousarray(np.asarray(L0_ffn_router_w, dtype=np.float32))
    rb = np.asarray(L0_ffn_router_b, dtype=np.float32)
    w1 = np.ascontiguousarray(np.asarray(L0_ffn_w1, dtype=np.float32))
    b1 = np.asarray(L0_ffn_b1, dtype=np.float32)
    w2 = np.ascontiguousarray(np.asarray(L0_ffn_w2, dtype=np.float32))
    b2 = np.ascontiguousarray(np.asarray(L0_ffn_b2, dtype=np.float32))

    wqt = np.ascontiguousarray(wq.T)
    wkt = np.ascontiguousarray(wk.T)
    wvt = np.ascontiguousarray(wv.T)
    wot = np.ascontiguousarray(wo.T)
    rbbc = np.ascontiguousarray(np.broadcast_to(rb[None, :], (128, NE)))
    b1r = np.ascontiguousarray(b1.reshape(NE, 32, 128).transpose(0, 2, 1))
    ident = np.eye(128, dtype=ml_dtypes.bfloat16)
    gsel = np.zeros((NE, NE * 128), dtype=ml_dtypes.bfloat16)
    for e in range(NE):
        gsel[e, e * 128:(e + 1) * 128] = ml_dtypes.bfloat16(1.0)
    masku = np.ascontiguousarray(np.triu(np.ones((128, 128), dtype=np.float32)))

    nc = _get_nc()
    xp = np.ascontiguousarray(x[0:7 * T])
    in_maps = []
    for c in range(N_CORES):
        pm = np.zeros((128, NE), dtype=np.float32)
        pm[:, :c] = 1.0
        in_maps.append(dict(
            x=np.ascontiguousarray(x[c * T:(c + 1) * T]),
            xp=xp,
            wqt=wqt, wkt=wkt, wvt=wvt, wot=wot,
            rw=rw, rbbc=rbbc, w1=w1, b1r=b1r, w2=w2, b2=b2,
            ident=ident, gsel=gsel, masku=masku, pmp=pm,
        ))
    res = run_bass_kernel_spmd(nc, in_maps, core_ids=list(range(N_CORES)))
    if DEBUG:
        kernel._last = res
    out = np.concatenate([res.results[c]["out"] for c in range(N_CORES)], axis=0)
    return out.reshape(1, 16384, 1024)


# version 14
# speedup vs baseline: 1.0276x; 1.0276x over previous
import numpy as np
import ml_dtypes
import concourse.bass as bass
import concourse.mybir as mybir
from concourse.bass_utils import run_bass_kernel_spmd
from contextlib import ExitStack

F32 = mybir.dt.float32
BF16 = mybir.dt.bfloat16
AF = mybir.ActivationFunctionType
ALU = mybir.AluOpType
AX = mybir.AxisListType

D = 1024
DC = 8
K = 64
T = 2048
NT = 16
NTQ = 4
NE = 8
FF = 4096
NG = 4
FCG = 8
N_CORES = 8
EPS_LN = 1e-5
EPS_DEN = 1e-6

DEBUG = False


def build_program():
    nc = bass.Bass("TRN2", target_bir_lowering=False, debug=False)

    x_d = nc.dram_tensor("x", [T, D], F32, kind="ExternalInput").ap()
    wqt_d = nc.dram_tensor("wqt", [D, K], F32, kind="ExternalInput").ap()
    wkt_d = nc.dram_tensor("wkt", [D, K], F32, kind="ExternalInput").ap()
    wvt_d = nc.dram_tensor("wvt", [D, K], F32, kind="ExternalInput").ap()
    wot_d = nc.dram_tensor("wot", [K, D], F32, kind="ExternalInput").ap()
    rw_d = nc.dram_tensor("rw", [D, NE], F32, kind="ExternalInput").ap()
    rbbc_d = nc.dram_tensor("rbbc", [128, NE], F32, kind="ExternalInput").ap()
    w1_d = nc.dram_tensor("w1", [NE, D, FF], F32, kind="ExternalInput").ap()
    b1r_d = nc.dram_tensor("b1r", [NE, 128, 32], F32, kind="ExternalInput").ap()
    w2_d = nc.dram_tensor("w2", [NE, FF, D], F32, kind="ExternalInput").ap()
    b2_d = nc.dram_tensor("b2", [NE, D], F32, kind="ExternalInput").ap()
    ident_d = nc.dram_tensor("ident", [128, 128], BF16, kind="ExternalInput").ap()
    gsel_d = nc.dram_tensor("gsel", [NE, NE * 128], BF16, kind="ExternalInput").ap()
    masku_d = nc.dram_tensor("masku", [128, 128], F32, kind="ExternalInput").ap()
    pmask_d = nc.dram_tensor("pmask", [K, NE], F32, kind="ExternalInput").ap()
    out_d = nc.dram_tensor("out", [T, D], F32, kind="ExternalOutput").ap()
    cc_in = nc.dram_tensor("cc_in", [K, K + 1], F32)
    cc_out = nc.dram_tensor("cc_out", [N_CORES * K, K + 1], F32, addr_space="Shared")
    if DEBUG:
        dbg_q = nc.dram_tensor("dbg_q", [K, T], F32, kind="ExternalOutput").ap()
        dbg_k = nc.dram_tensor("dbg_k", [K, T], F32, kind="ExternalOutput").ap()
        dbg_sini = nc.dram_tensor("dbg_sini", [K, K + 1], F32, kind="ExternalOutput").ap()
        dbg_resa = nc.dram_tensor("dbg_resa", [T, D], F32, kind="ExternalOutput").ap()
        dbg_gates = nc.dram_tensor("dbg_gates", [128, NT * NE], F32, kind="ExternalOutput").ap()
        dbg_xfm = nc.dram_tensor("dbg_xfm", [128, DC * T], F32, kind="ExternalOutput").ap()
        dbg_res0 = nc.dram_tensor("dbg_res0", [T, D], F32, kind="ExternalOutput").ap()
        dbg_stat = nc.dram_tensor("dbg_stat", [128, 16], F32, kind="ExternalOutput").ap()
        dbg_xh = nc.dram_tensor("dbg_xh", [128, D], F32, kind="ExternalOutput").ap()
        dbg_sl = nc.dram_tensor("dbg_sl", [K, K + 1], F32, kind="ExternalOutput").ap()
        dbg_vtm = nc.dram_tensor("dbg_vtm", [128, NT * (K + 1)], F32, kind="ExternalOutput").ap()
        dbg_ktm = nc.dram_tensor("dbg_ktm", [128, NT * K], F32, kind="ExternalOutput").ap()

    ctx = ExitStack()
    E_ = ctx.enter_context
    res = E_(nc.sbuf_tensor("res", [128, NT * D], F32))
    xfm = E_(nc.sbuf_tensor("xfm", [128, DC * T], BF16))
    h = E_(nc.sbuf_tensor("h", [128, FCG * T], BF16))
    w1b = E_(nc.sbuf_tensor("w1b", [128, DC * 1024], BF16))
    w2b = E_(nc.sbuf_tensor("w2b", [128, FCG * 1024], BF16))
    gbc = E_(nc.sbuf_tensor("gbc", [128, T], BF16))
    hscr = [E_(nc.sbuf_tensor(f"hscr{i}", [128, 512], BF16)) for i in range(4)]
    qfm = E_(nc.sbuf_tensor("qfm", [K, T], BF16))
    kfm = E_(nc.sbuf_tensor("kfm", [K, T], BF16))
    ktm = E_(nc.sbuf_tensor("ktm", [128, NT * K], BF16))
    vtm = E_(nc.sbuf_tensor("vtm", [128, NT * (K + 1)], BF16))
    wqT = E_(nc.sbuf_tensor("wqT", [128, DC * K], BF16))
    wkT = E_(nc.sbuf_tensor("wkT", [128, DC * K], BF16))
    wvT = E_(nc.sbuf_tensor("wvT", [128, DC * K], BF16))
    woT = E_(nc.sbuf_tensor("woT", [K, D], BF16))
    rw_s = E_(nc.sbuf_tensor("rw_s", [128, DC * NE], BF16))
    rbbc = E_(nc.sbuf_tensor("rbbc_s", [128, NE], F32))
    b1s = E_(nc.sbuf_tensor("b1s", [128, NE * 32], F32))
    b2s = E_(nc.sbuf_tensor("b2s", [NE, D], BF16))
    ident = E_(nc.sbuf_tensor("ident_s", [128, 128], BF16))
    gsel = E_(nc.sbuf_tensor("gsel_s", [NE, NE * 128], BF16))
    masku = E_(nc.sbuf_tensor("masku_s", [128, 128], F32))
    pmask = E_(nc.sbuf_tensor("pmask_s", [K, NE], F32))
    gtT = E_(nc.sbuf_tensor("gtT", [NE, T], BF16))
    scr = E_(nc.sbuf_tensor("scr", [128, 16], F32))
    sAll = E_(nc.sbuf_tensor("sAll", [K, NT * (K + 1)], BF16))
    xh = E_(nc.sbuf_tensor("xh", [128, D], BF16))
    eluA = E_(nc.sbuf_tensor("eluA", [K, 512], F32))
    eluB = E_(nc.sbuf_tensor("eluB", [K, 512], F32))
    bm = [E_(nc.sbuf_tensor(f"bm{i}", [128, 128], BF16)) for i in range(2)]
    otm = [E_(nc.sbuf_tensor(f"otm{i}", [128, K], BF16)) for i in range(2)]
    ofm = [E_(nc.sbuf_tensor(f"ofm{i}", [K, 128], BF16)) for i in range(2)]
    sA = E_(nc.sbuf_tensor("sA", [K, K + 1], BF16))
    sL = E_(nc.sbuf_tensor("sL", [K, K + 1], F32))
    g8 = E_(nc.sbuf_tensor("g8", [K, NE * (K + 1)], F32))
    sini = E_(nc.sbuf_tensor("sini", [K, K + 1], F32))
    sini_b = E_(nc.sbuf_tensor("sini_b", [K, K + 1], BF16))
    stat = E_(nc.sbuf_tensor("stat", [128, 16], F32))
    lg = E_(nc.sbuf_tensor("lg", [128, NE], F32))
    mx8 = E_(nc.sbuf_tensor("mx8", [128, NE], F32))
    e8 = E_(nc.sbuf_tensor("e8", [128, NE], F32))
    sel8 = E_(nc.sbuf_tensor("sel8", [128, NE], F32))
    gu8 = E_(nc.sbuf_tensor("gu8", [128, NE], F32))
    gts = E_(nc.sbuf_tensor("gts", [128, NE], BF16))

    PS = [E_(nc.psum_tensor(f"PS{i}", [128, 512], F32)) for i in range(8)]
    PSB = [p.ap().bitcast(BF16) for p in PS]

    SEM = {}
    for k in ("t", "v", "s", "g", "y", "cc"):
        SEM[k] = E_(nc.semaphore("sem_" + k))
    block = E_(nc.Block())

    Q = {k: [] for k in ("t", "v", "s", "g", "y")}
    C = {k: 0 for k in ("t", "v", "s", "g", "y", "cc")}
    WAITED = {k: {} for k in ("t", "v", "s", "g", "y")}

    def op(eng, f, inc=0, key=None):
        if inc:
            key = key or eng
            C[key] += inc
            def g_(E, f=f, key=key, inc=inc):
                f(E).then_inc(SEM[key], inc)
            Q[eng].append(g_)
            return C[key]
        Q[eng].append(f)
        return None

    def wt(eng, key, val):
        if not val:
            return
        assert val <= C[key], f"forward wait {eng} on {key}: {val} > {C[key]}"
        if WAITED[eng].get(key, -1) >= val:
            return
        WAITED[eng][key] = val
        Q[eng].append(lambda E, key=key, val=val: E.wait_ge(SEM[key], val))

    n_x = op("y", lambda E: E.dma_start(
        out=res[:, :].rearrange("p (tt d) -> p tt d", tt=NT),
        in_=x_d.rearrange("(tt p) d -> p tt d", p=128)), inc=16)

    def dma_w1(e, g):
        return op("g", lambda E, e=e, g=g: E.dma_start(
            out=w1b[:, :].rearrange("p (c f) -> p c f", c=DC),
            in_=w1_d[e, :, g * 1024:(g + 1) * 1024].rearrange("(c p) f -> p c f", p=128)), inc=16)

    def dma_w2(e, g):
        return op("g", lambda E, e=e, g=g: E.dma_start(
            out=w2b[:, :].rearrange("p (c d) -> p c d", c=FCG),
            in_=w2_d[e, g * 1024:(g + 1) * 1024, :].rearrange("(c p) d -> p c d", p=128)), inc=16)

    n_w1 = dma_w1(0, 0)
    n_w2 = dma_w2(0, 0)
    op("g", lambda E: E.dma_start(out=wqT[:, :].rearrange("p (c k) -> p c k", c=DC),
                                  in_=wqt_d.rearrange("(c p) k -> p c k", p=128)), inc=16)
    op("g", lambda E: E.dma_start(out=wkT[:, :].rearrange("p (c k) -> p c k", c=DC),
                                  in_=wkt_d.rearrange("(c p) k -> p c k", p=128)), inc=16)
    op("g", lambda E: E.dma_start(out=wvT[:, :].rearrange("p (c k) -> p c k", c=DC),
                                  in_=wvt_d.rearrange("(c p) k -> p c k", p=128)), inc=16)
    op("g", lambda E: E.dma_start(out=woT[:, :], in_=wot_d), inc=16)
    op("g", lambda E: E.dma_start(out=rw_s[:, :].rearrange("p (c r) -> p c r", c=DC),
                                  in_=rw_d.rearrange("(c p) r -> p c r", p=128)), inc=16)
    op("g", lambda E: E.dma_start(out=rbbc[:, :], in_=rbbc_d), inc=16)
    op("g", lambda E: E.dma_start(out=b1s[:, :].rearrange("p (e f) -> p e f", e=NE),
                                  in_=b1r_d.rearrange("e p f -> p e f")), inc=16)
    op("g", lambda E: E.dma_start(out=b2s[:, :], in_=b2_d), inc=16)
    op("g", lambda E: E.dma_start(out=ident[:, :], in_=ident_d), inc=16)
    op("g", lambda E: E.dma_start(out=gsel[:, :], in_=gsel_d), inc=16)
    op("g", lambda E: E.dma_start(out=masku[:, :], in_=masku_d), inc=16)
    n_init_dma = op("g", lambda E: E.dma_start(out=pmask[:, :], in_=pmask_d), inc=16)

    op("v", lambda E: E.memset(vtm[:, :], 1.0))

    wt("t", "g", n_init_dma)
    for b in range(8):
        op("t", lambda E, b=b: E.transpose(PSB[b][:, 0:128], ident[:, :],
                                           ident[:, :]), inc=1)
    n_warm = C["t"]
    wt("v", "t", n_warm)
    nwv = op("v", lambda E: E.tensor_copy(stat[:, 8:9], PSB[0][0:128, 0:1]), inc=1)
    wt("t", "v", nwv)
    wt("s", "t", n_warm)

    tr_copy_done = {}
    state = dict(prev_nxh=0, prev_tr_t=0, nex_prev=0)

    def emit_ln(i):
        sl = res[:, i * D:(i + 1) * D]
        wt("v", "y", n_x)
        wt("s", "y", n_x)
        wt("s", "v", state.get("prev_stat_rd", 0))
        op("s", lambda E, sl=sl: E.activation(PS[7][:, :], sl[:, 0:512], AF.Square,
                                              accum_out=stat[:, 14:15]))
        op("s", lambda E, sl=sl: E.activation(PS[7][:, :], sl[:, 512:1024], AF.Square,
                                              accum_out=stat[:, 1:2]))
        nsq = op("s", lambda E: E.copy(stat[:, 6:7], stat[:, 1:2]), inc=1)
        n2 = op("v", lambda E, sl=sl: E.reduce_sum(stat[:, 0:1], sl, axis=AX.X), inc=1)
        wt("v", "v", n2)
        wt("v", "s", nsq)
        op("v", lambda E: E.tensor_scalar(stat[:, 2:3], stat[:, 0:1], 1.0 / D, None, ALU.mult))
        ng1 = op("v", lambda E: E.scalar_tensor_tensor(stat[:, 15:16], stat[:, 1:2], 1.0,
                                                       stat[:, 14:15], ALU.mult, ALU.add), inc=1)
        state["prev_stat_rd"] = ng1
        wt("v", "v", ng1)
        op("v", lambda E: E.scalar_tensor_tensor(stat[:, 4:5], stat[:, 2:3], 1.0,
                                                 stat[:, 2:3], ALU.mult, ALU.mult))
        ng2 = op("v", lambda E: E.tensor_scalar(stat[:, 3:4], stat[:, 15:16], 1.0 / D, EPS_LN,
                                                ALU.mult, ALU.add), inc=1)
        wt("v", "v", ng2)
        nv = op("v", lambda E: E.tensor_tensor(stat[:, 5:6], stat[:, 3:4], stat[:, 4:5],
                                               ALU.subtract), inc=1)
        wt("s", "v", nv)
        nln = op("s", lambda E: E.activation(stat[:, 6:7], stat[:, 5:6], AF.Ln), inc=1)
        wt("s", "s", nln)
        op("s", lambda E: E.activation(stat[:, 7:8], stat[:, 6:7], AF.Exp, scale=-0.5))
        ns = op("s", lambda E: E.copy(stat[:, 6:7], stat[:, 7:8]), inc=1)
        wt("v", "s", ns)
        wt("v", "t", state["prev_tr_t"])
        op("v", lambda E, sl=sl: E.tensor_scalar(
            xh[:, :], sl, stat[:, 2:3], stat[:, 7:8], ALU.subtract, ALU.mult))
        nxh = op("v", lambda E: E.tensor_copy(stat[:, 8:9], stat[:, 7:8]), inc=1)
        state["prev_nxh"] = nxh
        return nxh

    def emit_transposes(i, nxh, banks=(3, 4)):
        wt("t", "v", nxh)
        wt("t", "g", n_init_dma)
        last_t = 0
        for cp in range(DC // 2):
            b = banks[cp % 2]
            key = ("tr", b)
            wt("t", "v", tr_copy_done.get(key, 0))
            for half in range(2):
                c = cp * 2 + half
                ntr = op("t", lambda E, c=c, b=b, half=half: E.transpose(
                    PSB[b][:, half * 128:(half + 1) * 128],
                    xh[:, c * 128:(c + 1) * 128], ident[:, :]),
                    inc=(1 if half == 1 else 0))
            ntr = C["t"]
            wt("v", "t", ntr)
            c0 = cp * 2
            ncp = op("v", lambda E, c0=c0, b=b, i=i: E.tensor_copy(
                xfm[:, :].rearrange("p (c t) -> p c t", c=DC)[:, c0:c0 + 2,
                                                             i * 128:(i + 1) * 128],
                PSB[b][:, 0:256].rearrange("p (c t) -> p c t", c=2)), inc=1)
            tr_copy_done[key] = ncp
            last_t = ntr
        state["prev_tr_t"] = last_t
        return C["v"]

    kv_ready = [0] * NT
    ktr_copy = [0] * NT
    qk_read = {}
    nfm_last = 0

    for i in range(NT):
        nxh = emit_ln(i)
        if DEBUG and i == 0:
            wt("g", "v", nxh)
            op("g", lambda E: E.dma_start(out=dbg_stat, in_=stat[:, :]), inc=16)
            op("g", lambda E: E.dma_start(out=dbg_xh, in_=xh[:, :]), inc=16)
            op("y", lambda E: E.dma_start(
                out=dbg_res0.rearrange("(tt p) d -> p tt d", p=128),
                in_=res[:, :].rearrange("p (tt d) -> p tt d", tt=NT)), inc=16, key="y")
        n_xfm_i = emit_transposes(i, nxh)

        wt("t", "g", n_init_dma)
        wt("t", "v", n_xfm_i)
        wt("t", "v", qk_read.get("vps", 0))
        for c in range(DC):
            op("t", lambda E, c=c, i=i: E.matmul(
                PS[2][0:128, 0:K],
                xfm[:, c * T + i * 128: c * T + i * 128 + 128],
                wvT[:, c * K:(c + 1) * K],
                start=(c == 0), stop=(c == DC - 1)),
                inc=(1 if c == DC - 1 else 0))
        n_vmm = C["t"]
        wt("v", "t", n_vmm)
        nvc = op("v", lambda E, i=i: E.tensor_copy(
            vtm[:, i * (K + 1): i * (K + 1) + K], PS[2][0:128, 0:K]), inc=1)
        qk_read["vps"] = nvc

        if i % 4 == 3:
            tq = i // 4
            for (psb, wT, dstfm) in ((0, wqT, qfm), (1, wkT, kfm)):
                keyq = ("qkps", psb)
                wt("t", "v", n_xfm_i)
                wt("t", "v", qk_read.get(keyq, 0))
                for c in range(DC):
                    op("t", lambda E, c=c, tq=tq, psb=psb, wT=wT: E.matmul(
                        PS[psb][0:K, :],
                        wT[:, c * K:(c + 1) * K],
                        xfm[:, c * T + tq * 512: c * T + (tq + 1) * 512],
                        start=(c == 0), stop=(c == DC - 1)),
                        inc=(1 if c == DC - 1 else 0))
                nqk = C["t"]
                wt("v", "t", nqk)
                op("v", lambda E, psb=psb: E.tensor_scalar(
                    eluA[:, :], PS[psb][0:K, :], 0.0, None, ALU.min))
                nrd = op("v", lambda E, psb=psb: E.tensor_scalar(
                    eluB[:, :], PS[psb][0:K, :], 0.0, None, ALU.max), inc=1)
                qk_read[keyq] = nrd
                wt("s", "v", nrd)
                nex = op("s", lambda E: E.activation(eluA[:, :], eluA[:, :], AF.Exp), inc=1)
                wt("v", "s", nex)
                wt("v", "v", nrd)
                nfm_last = op("v", lambda E, dstfm=dstfm, tq=tq: E.tensor_tensor(
                    dstfm[:, tq * 512:(tq + 1) * 512], eluA[:, :], eluB[:, :], ALU.add), inc=1)
            for ii in range(tq * 4, tq * 4 + 4):
                wt("t", "v", nfm_last)
                wt("t", "v", ktr_copy[ii - 1] if ii > 0 else 0)
                nkt = op("t", lambda E, ii=ii: E.transpose(
                    PSB[5][0:128, 0:K], kfm[:, ii * 128:(ii + 1) * 128],
                    ident[0:K, 0:K]), inc=1)
                wt("v", "t", nkt)
                nkc = op("v", lambda E, ii=ii: E.tensor_copy(
                    ktm[:, ii * K:(ii + 1) * K], PSB[5][0:128, 0:K]), inc=1)
                ktr_copy[ii] = nkc
                kv_ready[ii] = nkc
            for ii in range(tq * 4, tq * 4 + 4):
                wt("t", "v", kv_ready[ii])
                op("t", lambda E, ii=ii: E.matmul(
                    PS[6][0:K, 0:K + 1],
                    ktm[:, ii * K:(ii + 1) * K],
                    vtm[:, ii * (K + 1):(ii + 1) * (K + 1)],
                    start=(ii == 0), stop=(ii == NT - 1)),
                    inc=(1 if ii == NT - 1 else 0))
    n_sloc = C["t"]

    if DEBUG:
        wt("g", "v", C["v"])
        wt("g", "s", C["s"])
        op("g", lambda E: E.dma_start(out=dbg_xfm, in_=xfm[:, :]), inc=16)
        op("g", lambda E: E.dma_start(out=dbg_q, in_=qfm[:, :]), inc=16)
        op("g", lambda E: E.dma_start(out=dbg_k, in_=kfm[:, :]), inc=16)
        op("g", lambda E: E.dma_start(out=dbg_vtm, in_=vtm[:, :]), inc=16)
        op("g", lambda E: E.dma_start(out=dbg_ktm, in_=ktm[:, :]), inc=16)
    wt("v", "t", n_sloc)
    n_sl = op("v", lambda E: E.tensor_copy(sL[:, :], PS[6][0:K, 0:K + 1]), inc=1)
    wt("g", "v", n_sl)
    if DEBUG:
        op("g", lambda E: E.dma_start(out=dbg_sl, in_=sL[:, :]), inc=16)
    n_st = op("g", lambda E: E.dma_start(out=cc_in.ap(), in_=sL[:, :]), inc=16)
    wt("g", "g", n_st)
    n_cc = op("g", lambda E: E.collective_compute(
        "AllGather", ALU.bypass, replica_groups=[list(range(N_CORES))],
        ins=[cc_in.ap().opt()], outs=[cc_out.ap().opt()]), inc=1, key="cc")
    wt("g", "cc", n_cc)
    n_g8 = op("g", lambda E: E.dma_start(
        out=g8[:, :].rearrange("k (r m) -> k r m", r=NE),
        in_=cc_out.ap().rearrange("(r k) m -> k r m", k=K)), inc=16)
    wt("v", "g", n_g8)
    nacc = op("v", lambda E: E.memset(sini[:, :], 0.0), inc=1)
    for r in range(NE):
        wt("v", "v", nacc)
        nacc = op("v", lambda E, r=r: E.scalar_tensor_tensor(
            sini[:, :], g8[:, r * (K + 1):(r + 1) * (K + 1)], pmask[:, r:r + 1],
            sini[:, :], ALU.mult, ALU.add), inc=1)
    wt("v", "v", nacc)
    n_sini = op("v", lambda E: E.tensor_copy(sini_b[:, :], sini[:, :]), inc=1)
    if DEBUG:
        wt("g", "v", n_sini)
        op("g", lambda E: E.dma_start(out=dbg_sini, in_=sini[:, :]), inc=16)

    wt("t", "v", n_sini)
    wt("t", "v", C["v"])
    wt("t", "s", C["s"])
    nupd = op("t", lambda E: E.matmul(PS[6][0:K, 0:K + 1], ident[0:K, 0:K],
                                      sini_b[:, :], start=True, stop=False), inc=1)
    wt("v", "t", nupd)
    ncopy = op("v", lambda E: E.tensor_copy(
        sAll[:, 0:K + 1], PS[6][0:K, 0:K + 1]), inc=1)
    for j in range(1, NT):
        wt("t", "v", ncopy)
        nupd = op("t", lambda E, j=j: E.matmul(
            PS[6][0:K, 0:K + 1], ktm[:, (j - 1) * K:j * K],
            vtm[:, (j - 1) * (K + 1):j * (K + 1)], start=False, stop=False), inc=1)
        wt("v", "t", nupd)
        ncopy = op("v", lambda E, j=j: E.tensor_copy(
            sAll[:, j * (K + 1):(j + 1) * (K + 1)], PS[6][0:K, 0:K + 1]), inc=1)
    n_states = ncopy
    bm_done = [0, 0]
    bm_read = [0, 0]
    numused = [0, 0]
    scorein = [0, 0]
    ot_copy = [0, 0]
    ofm_read = [0, 0]
    otm_read = [0, 0]
    ao_add = [0, 0]
    res_attn_done = [0] * NT

    for j in range(NT):
        sb = j % 2
        wt("t", "v", scorein[sb])
        nB = op("t", lambda E, j=j, sb=sb: E.matmul(
            PS[sb][:, 0:128], kfm[:, j * 128:(j + 1) * 128],
            qfm[:, j * 128:(j + 1) * 128], start=True, stop=True), inc=1)
        wt("v", "t", nB)
        wt("v", "t", bm_read[sb])
        nbm = op("v", lambda E, sb=sb: E.tensor_tensor(
            bm[sb][:, :], PS[sb][:, 0:128], masku[:, :], ALU.mult), inc=1)
        bm_done[sb] = nbm
        scorein[sb] = nbm
        wt("t", "v", n_states)
        wt("t", "v", numused[sb])
        op("t", lambda E, j=j, sb=sb: E.matmul(
            PS[2 + sb][:, 0:K + 1], qfm[:, j * 128:(j + 1) * 128],
            sAll[:, j * (K + 1):(j + 1) * (K + 1)], start=True, stop=False))
        wt("t", "v", bm_done[sb])
        nnum = op("t", lambda E, j=j, sb=sb: E.matmul(
            PS[2 + sb][:, 0:K + 1], bm[sb][:, :],
            vtm[:, j * (K + 1):(j + 1) * (K + 1)], start=False, stop=True), inc=1)
        bm_read[sb] = nnum
        wt("v", "t", nnum)
        nd1 = op("v", lambda E, sb=sb: E.tensor_scalar(
            stat[:, 9:10], PS[2 + sb][:, K:K + 1], EPS_DEN, None, ALU.add), inc=1)
        wt("v", "v", nd1)
        nd2 = op("v", lambda E: E.reciprocal(stat[:, 10:11], stat[:, 9:10]), inc=1)
        wt("v", "v", nd2)
        wt("v", "t", otm_read[sb])
        not_ = op("v", lambda E, sb=sb: E.tensor_scalar(
            otm[sb][:, :], PS[2 + sb][:, 0:K], stat[:, 10:11], None, ALU.mult), inc=1)
        numused[sb] = not_
        wt("t", "v", not_)
        wt("t", "v", ot_copy[sb])
        ntr = op("t", lambda E, sb=sb: E.transpose(
            PSB[4][0:K, sb * 128:(sb + 1) * 128], otm[sb][:, :], ident[:, :]), inc=1)
        otm_read[sb] = ntr
        wt("v", "t", ntr)
        wt("v", "t", ofm_read[sb])
        ncp = op("v", lambda E, sb=sb: E.tensor_copy(
            ofm[sb][:, :], PSB[4][0:K, sb * 128:(sb + 1) * 128]), inc=1)
        ot_copy[sb] = ncp
        wt("t", "v", ncp)
        for half, bank in ((0, 5), (1, 7)):
            wt("t", "v", ao_add[half])
            nao = op("t", lambda E, sb=sb, half=half, bank=bank: E.matmul(
                PS[bank][:, :], ofm[sb][:, :], woT[:, half * 512:(half + 1) * 512],
                start=True, stop=True), inc=1)
            ofm_read[sb] = nao
            wt("v", "t", nao)
            nadd = op("v", lambda E, j=j, half=half, bank=bank: E.tensor_tensor(
                res[:, j * D + half * 512: j * D + (half + 1) * 512],
                PS[bank][:, :],
                res[:, j * D + half * 512: j * D + (half + 1) * 512], ALU.add), inc=1)
            ao_add[half] = nadd
        res_attn_done[j] = ao_add[1]

    n_dbga = 0
    if DEBUG:
        wt("y", "v", C["v"])
        n_dbga = op("y", lambda E: E.dma_start(
            out=dbg_resa.rearrange("(tt p) d -> p tt d", p=128),
            in_=res[:, :].rearrange("p (tt d) -> p tt d", tt=NT)), inc=16, key="y")

    wt("t", "v", C["v"])
    wt("t", "s", C["s"])
    wt("s", "t", C["t"])
    wt("s", "v", C["v"])
    gts_read = 0
    lg_read = 0
    gt_copy = 0
    gates_done = 0
    for i in range(NT):
        nxh = emit_ln(i)
        n_xfm_i = emit_transposes(i, nxh, banks=(3, 4))
        wt("t", "v", n_xfm_i)
        wt("t", "v", lg_read)
        for c in range(DC):
            op("t", lambda E, c=c, i=i: E.matmul(
                PS[2][:, 0:NE],
                xfm[:, c * T + i * 128: c * T + i * 128 + 128],
                rw_s[:, c * NE:(c + 1) * NE],
                start=(c == 0), stop=(c == DC - 1)),
                inc=(1 if c == DC - 1 else 0))
        nlg = C["t"]
        wt("v", "t", nlg)
        wt("v", "s", state["nex_prev"])
        lg_read = op("v", lambda E: E.tensor_tensor(lg[:, :], PS[2][:, 0:NE],
                                                    rbbc[:, :], ALU.add), inc=1)
        wt("v", "v", lg_read)
        nmx = op("v", lambda E: E.max(out=mx8[:, :], in_=lg[:, :]), inc=1)
        wt("v", "v", nmx)
        op("v", lambda E: E.tensor_scalar(stat[:, 11:12], mx8[:, 0:1], -1.0, None, ALU.mult))
        nsl = op("v", lambda E: E.tensor_scalar(sel8[:, :], lg[:, :], mx8[:, 1:2],
                                                None, ALU.is_ge), inc=1)
        wt("s", "v", nsl)
        op("s", lambda E: E.activation(e8[:, :], lg[:, :], AF.Exp,
                                       bias=stat[:, 11:12]))
        nex = op("s", lambda E: E.copy(scr[:, 0:1], e8[:, 0:1]), inc=1)
        state["nex_prev"] = nex
        wt("v", "s", nex)
        wt("v", "v", nsl)
        ngu = op("v", lambda E: E.tensor_tensor(gu8[:, :], e8[:, :], sel8[:, :], ALU.mult), inc=1)
        wt("v", "v", ngu)
        ngs = op("v", lambda E: E.reduce_sum(stat[:, 12:13], gu8[:, :], axis=AX.X), inc=1)
        wt("v", "v", ngs)
        nrc = op("v", lambda E: E.reciprocal(stat[:, 13:14], stat[:, 12:13]), inc=1)
        wt("v", "v", nrc)
        wt("v", "t", gts_read)
        wt("v", "g", state.get("dbg_g", 0))
        ngt = op("v", lambda E: E.tensor_scalar(gts[:, :], gu8[:, :], stat[:, 13:14],
                                                None, ALU.mult), inc=1)
        if DEBUG:
            wt("g", "v", ngt)
            state["dbg_g"] = op("g", lambda E, i=i: E.dma_start(
                out=dbg_gates[:, i * NE:(i + 1) * NE], in_=gts[:, :]), inc=16)
        wt("t", "v", ngt)
        wt("t", "v", gt_copy)
        ngtr = op("t", lambda E: E.transpose(PSB[5][0:NE, 0:128], gts[:, :],
                                             ident[:, :]), inc=1)
        gts_read = ngtr
        wt("v", "t", ngtr)
        gt_copy = op("v", lambda E, i=i: E.tensor_copy(gtT[:, i * 128:(i + 1) * 128],
                                                       PSB[5][0:NE, 0:128]), inc=1)
        gates_done = gt_copy

    wt("t", "v", C["v"])
    wt("t", "s", C["s"])
    if DEBUG:
        wt("v", "y", n_dbga)
    hps_free = {0: 0, 1: 0}
    gm_tq = [0, 0, 0, 0]
    hscr_free = [0, 0, 0, 0]
    out_free = {(0, 0): 0, (0, 1): 0, (1, 0): 0, (1, 1): 0}
    gbc_copy = 0
    last_gm = 0
    w1_ready = n_w1
    w2_ready = n_w2

    for e in range(NE):
        wt("t", "v", gates_done)
        for half in range(4):
            wt("t", "v", gbc_copy)
            ngb = op("t", lambda E, e=e, half=half: E.matmul(
                PS[6][:, :], gsel[:, e * 128:(e + 1) * 128],
                gtT[:, half * 512:(half + 1) * 512], start=True, stop=True), inc=1)
            wt("v", "t", ngb)
            wt("v", "v", last_gm)
            gbc_copy = op("v", lambda E, half=half: E.tensor_copy(
                gbc[:, half * 512:(half + 1) * 512], PS[6][:, :]), inc=1)
        for g in range(NG):
            wt("t", "g", w1_ready)
            kidx = 0
            for fc in range(FCG):
                for tq in range(NTQ):
                    hb = kidx % 2
                    wt("t", "s", hps_free[hb])
                    for c in range(DC):
                        op("t", lambda E, c=c, fc=fc, tq=tq, hb=hb: E.matmul(
                            PS[hb][:, :],
                            w1b[:, c * 1024 + fc * 128: c * 1024 + fc * 128 + 128],
                            xfm[:, c * T + tq * 512: c * T + (tq + 1) * 512],
                            start=(c == 0), stop=(c == DC - 1)),
                            inc=(1 if c == DC - 1 else 0))
                    nmm = C["t"]
                    sc = kidx % 4
                    wt("s", "t", nmm)
                    wt("s", "v", hscr_free[sc])
                    ngl = op("s", lambda E, hb=hb, e=e, g=g, fc=fc, sc=sc: E.activation(
                        hscr[sc][:, :], PS[hb][:, :], AF.Gelu,
                        bias=b1s[:, e * 32 + g * FCG + fc: e * 32 + g * FCG + fc + 1]), inc=1)
                    hps_free[hb] = ngl
                    wt("v", "s", ngl)
                    wt("v", "v", gbc_copy)
                    ngm = op("v", lambda E, fc=fc, tq=tq, sc=sc: E.tensor_tensor(
                        h[:, fc * T + tq * 512: fc * T + (tq + 1) * 512],
                        hscr[sc][:, :], gbc[:, tq * 512:(tq + 1) * 512], ALU.mult), inc=1)
                    hscr_free[sc] = ngm
                    if fc == FCG - 1:
                        gm_tq[tq] = ngm
                    kidx += 1
            h_written = C["v"]
            last_gm = h_written
            mm1_last = C["t"]
            nxt = (e, g + 1) if g + 1 < NG else ((e + 1, 0) if e + 1 < NE else None)
            if nxt is not None:
                wt("g", "t", mm1_last)
                w1_ready = dma_w1(*nxt)
            wt("t", "g", w2_ready)
            first = (e == 0 and g == 0)
            for s16 in range(NT):
                wt("t", "v", gm_tq[s16 // 4])
                slot = s16 % 2
                for dh in range(2):
                    bank = 2 + slot * 2 + dh
                    wt("t", "v", out_free[(slot, dh)])
                    if first:
                        op("t", lambda E, s16=s16, dh=dh, bank=bank: E.matmul(
                            PS[bank][:, :], gtT[:, s16 * 128:(s16 + 1) * 128],
                            b2s[:, dh * 512:(dh + 1) * 512], start=True, stop=False))
                    for fc in range(FCG):
                        op("t", lambda E, s16=s16, dh=dh, fc=fc, bank=bank, first=first: E.matmul(
                            PS[bank][:, :],
                            h[:, fc * T + s16 * 128: fc * T + s16 * 128 + 128],
                            w2b[:, fc * 1024 + dh * 512: fc * 1024 + dh * 512 + 512],
                            start=(fc == 0 and not first), stop=(fc == FCG - 1)),
                            inc=(1 if fc == FCG - 1 else 0))
                    nmm2 = C["t"]
                    wt("v", "t", nmm2)
                    nev = op("v", lambda E, s16=s16, dh=dh, bank=bank: E.tensor_tensor(
                        res[:, s16 * D + dh * 512: s16 * D + (dh + 1) * 512],
                        PS[bank][:, :],
                        res[:, s16 * D + dh * 512: s16 * D + (dh + 1) * 512],
                        ALU.add), inc=1)
                    out_free[(slot, dh)] = nev
                if e == NE - 1 and g == NG - 1:
                    wt("y", "v", out_free[(slot, 1)])
                    op("y", lambda E, s16=s16: E.dma_start(
                        out=out_d[s16 * 128:(s16 + 1) * 128, :],
                        in_=res[:, s16 * D:(s16 + 1) * D]), inc=16)
            mm2_last = C["t"]
            if nxt is not None:
                wt("g", "t", mm2_last)
                w2_ready = dma_w2(*nxt)

    wt("y", "y", C["y"])
    if DEBUG:
        wt("g", "g", C["g"])

    @block.gpsimd
    def _(En):
        for f in Q["g"]:
            f(En)

    @block.tensor
    def _(En):
        for f in Q["t"]:
            f(En)

    @block.vector
    def _(En):
        for f in Q["v"]:
            f(En)

    @block.scalar
    def _(En):
        for f in Q["s"]:
            f(En)

    @block.sync
    def _(En):
        for f in Q["y"]:
            f(En)

    ctx.close()
    return nc


_NC_CACHE = None


def _get_nc():
    global _NC_CACHE
    if _NC_CACHE is None:
        _NC_CACHE = build_program()
    return _NC_CACHE


def kernel(L0_attn_x, L0_attn_Wq, L0_attn_Wk, L0_attn_Wv, L0_attn_Wo,
           L0_ffn_router_w, L0_ffn_router_b, L0_ffn_w1, L0_ffn_b1,
           L0_ffn_w2, L0_ffn_b2):
    x = np.ascontiguousarray(np.asarray(L0_attn_x, dtype=np.float32).reshape(16384, 1024))
    wq = np.asarray(L0_attn_Wq, dtype=np.float32)
    wk = np.asarray(L0_attn_Wk, dtype=np.float32)
    wv = np.asarray(L0_attn_Wv, dtype=np.float32)
    wo = np.asarray(L0_attn_Wo, dtype=np.float32)
    rw = np.ascontiguousarray(np.asarray(L0_ffn_router_w, dtype=np.float32))
    rb = np.asarray(L0_ffn_router_b, dtype=np.float32)
    w1 = np.ascontiguousarray(np.asarray(L0_ffn_w1, dtype=np.float32))
    b1 = np.asarray(L0_ffn_b1, dtype=np.float32)
    w2 = np.ascontiguousarray(np.asarray(L0_ffn_w2, dtype=np.float32))
    b2 = np.ascontiguousarray(np.asarray(L0_ffn_b2, dtype=np.float32))

    wqt = np.ascontiguousarray(wq.T)
    wkt = np.ascontiguousarray(wk.T)
    wvt = np.ascontiguousarray(wv.T)
    wot = np.ascontiguousarray(wo.T)
    rbbc = np.ascontiguousarray(np.broadcast_to(rb[None, :], (128, NE)))
    b1r = np.ascontiguousarray(b1.reshape(NE, 32, 128).transpose(0, 2, 1))
    ident = np.eye(128, dtype=ml_dtypes.bfloat16)
    gsel = np.zeros((NE, NE * 128), dtype=ml_dtypes.bfloat16)
    for e in range(NE):
        gsel[e, e * 128:(e + 1) * 128] = ml_dtypes.bfloat16(1.0)
    masku = np.ascontiguousarray(np.triu(np.ones((128, 128), dtype=np.float32)))

    nc = _get_nc()
    in_maps = []
    for c in range(N_CORES):
        pm = np.zeros((K, NE), dtype=np.float32)
        pm[:, :c] = 1.0
        in_maps.append(dict(
            x=np.ascontiguousarray(x[c * T:(c + 1) * T]),
            wqt=wqt, wkt=wkt, wvt=wvt, wot=wot,
            rw=rw, rbbc=rbbc, w1=w1, b1r=b1r, w2=w2, b2=b2,
            ident=ident, gsel=gsel, masku=masku, pmask=pm,
        ))
    res = run_bass_kernel_spmd(nc, in_maps, core_ids=list(range(N_CORES)))
    if DEBUG:
        kernel._last = res
    out = np.concatenate([res.results[c]["out"] for c in range(N_CORES)], axis=0)
    return out.reshape(1, 16384, 1024)


# version 15
# speedup vs baseline: 1.0295x; 1.0018x over previous
import numpy as np
import ml_dtypes
import concourse.bass as bass
import concourse.mybir as mybir
from concourse.bass_utils import run_bass_kernel_spmd
from contextlib import ExitStack

F32 = mybir.dt.float32
BF16 = mybir.dt.bfloat16
AF = mybir.ActivationFunctionType
ALU = mybir.AluOpType
AX = mybir.AxisListType

D = 1024
DC = 8
K = 64
T = 2048
NT = 16
NTQ = 4
NE = 8
FF = 4096
NG = 4
FCG = 8
N_CORES = 8
EPS_LN = 1e-5
EPS_DEN = 1e-6

DEBUG = False


def build_program():
    nc = bass.Bass("TRN2", target_bir_lowering=False, debug=False)

    x_d = nc.dram_tensor("x", [T, D], F32, kind="ExternalInput").ap()
    wqt_d = nc.dram_tensor("wqt", [D, K], F32, kind="ExternalInput").ap()
    wkt_d = nc.dram_tensor("wkt", [D, K], F32, kind="ExternalInput").ap()
    wvt_d = nc.dram_tensor("wvt", [D, K], F32, kind="ExternalInput").ap()
    wot_d = nc.dram_tensor("wot", [K, D], F32, kind="ExternalInput").ap()
    rw_d = nc.dram_tensor("rw", [D, NE], F32, kind="ExternalInput").ap()
    rbbc_d = nc.dram_tensor("rbbc", [128, NE], F32, kind="ExternalInput").ap()
    w1_d = nc.dram_tensor("w1", [NE, D, FF], F32, kind="ExternalInput").ap()
    b1r_d = nc.dram_tensor("b1r", [NE, 128, 32], F32, kind="ExternalInput").ap()
    w2_d = nc.dram_tensor("w2", [NE, FF, D], F32, kind="ExternalInput").ap()
    b2_d = nc.dram_tensor("b2", [NE, D], F32, kind="ExternalInput").ap()
    ident_d = nc.dram_tensor("ident", [128, 128], BF16, kind="ExternalInput").ap()
    gsel_d = nc.dram_tensor("gsel", [NE, NE * 128], BF16, kind="ExternalInput").ap()
    masku_d = nc.dram_tensor("masku", [128, 128], F32, kind="ExternalInput").ap()
    pmask_d = nc.dram_tensor("pmask", [K, NE], F32, kind="ExternalInput").ap()
    out_d = nc.dram_tensor("out", [T, D], F32, kind="ExternalOutput").ap()
    cc_in = nc.dram_tensor("cc_in", [K, K + 1], F32)
    cc_out = nc.dram_tensor("cc_out", [N_CORES * K, K + 1], F32, addr_space="Shared")
    if DEBUG:
        dbg_q = nc.dram_tensor("dbg_q", [K, T], F32, kind="ExternalOutput").ap()
        dbg_k = nc.dram_tensor("dbg_k", [K, T], F32, kind="ExternalOutput").ap()
        dbg_sini = nc.dram_tensor("dbg_sini", [K, K + 1], F32, kind="ExternalOutput").ap()
        dbg_resa = nc.dram_tensor("dbg_resa", [T, D], F32, kind="ExternalOutput").ap()
        dbg_gates = nc.dram_tensor("dbg_gates", [128, NT * NE], F32, kind="ExternalOutput").ap()
        dbg_xfm = nc.dram_tensor("dbg_xfm", [128, DC * T], F32, kind="ExternalOutput").ap()
        dbg_res0 = nc.dram_tensor("dbg_res0", [T, D], F32, kind="ExternalOutput").ap()
        dbg_stat = nc.dram_tensor("dbg_stat", [128, 16], F32, kind="ExternalOutput").ap()
        dbg_xh = nc.dram_tensor("dbg_xh", [128, D], F32, kind="ExternalOutput").ap()
        dbg_sl = nc.dram_tensor("dbg_sl", [K, K + 1], F32, kind="ExternalOutput").ap()
        dbg_vtm = nc.dram_tensor("dbg_vtm", [128, NT * (K + 1)], F32, kind="ExternalOutput").ap()
        dbg_ktm = nc.dram_tensor("dbg_ktm", [128, NT * K], F32, kind="ExternalOutput").ap()

    ctx = ExitStack()
    E_ = ctx.enter_context
    res = E_(nc.sbuf_tensor("res", [128, NT * D], F32))
    xfm = E_(nc.sbuf_tensor("xfm", [128, DC * T], BF16))
    h = E_(nc.sbuf_tensor("h", [128, FCG * T], BF16))
    w1b = E_(nc.sbuf_tensor("w1b", [128, DC * 1024], BF16))
    w2b = E_(nc.sbuf_tensor("w2b", [128, FCG * 1024], BF16))
    gbc = E_(nc.sbuf_tensor("gbc", [128, T], BF16))
    hscr = [E_(nc.sbuf_tensor(f"hscr{i}", [128, 512], BF16)) for i in range(4)]
    qfm = E_(nc.sbuf_tensor("qfm", [K, T], BF16))
    kfm = E_(nc.sbuf_tensor("kfm", [K, T], BF16))
    ktm = E_(nc.sbuf_tensor("ktm", [128, NT * K], BF16))
    vtm = E_(nc.sbuf_tensor("vtm", [128, NT * (K + 1)], BF16))
    wqT = E_(nc.sbuf_tensor("wqT", [128, DC * K], BF16))
    wkT = E_(nc.sbuf_tensor("wkT", [128, DC * K], BF16))
    wvT = E_(nc.sbuf_tensor("wvT", [128, DC * K], BF16))
    woT = E_(nc.sbuf_tensor("woT", [K, D], BF16))
    rw_s = E_(nc.sbuf_tensor("rw_s", [128, DC * NE], BF16))
    rbbc = E_(nc.sbuf_tensor("rbbc_s", [128, NE], F32))
    b1s = E_(nc.sbuf_tensor("b1s", [128, NE * 32], F32))
    b2s = E_(nc.sbuf_tensor("b2s", [NE, D], BF16))
    ident = E_(nc.sbuf_tensor("ident_s", [128, 128], BF16))
    gsel = E_(nc.sbuf_tensor("gsel_s", [NE, NE * 128], BF16))
    masku = E_(nc.sbuf_tensor("masku_s", [128, 128], F32))
    pmask = E_(nc.sbuf_tensor("pmask_s", [K, NE], F32))
    gtT = E_(nc.sbuf_tensor("gtT", [NE, T], BF16))
    scr = E_(nc.sbuf_tensor("scr", [128, 16], F32))
    sAll = E_(nc.sbuf_tensor("sAll", [K, NT * (K + 1)], BF16))
    xh = E_(nc.sbuf_tensor("xh", [128, D], BF16))
    eluA = E_(nc.sbuf_tensor("eluA", [K, 512], F32))
    eluB = E_(nc.sbuf_tensor("eluB", [K, 512], F32))
    bm = [E_(nc.sbuf_tensor(f"bm{i}", [128, 128], BF16)) for i in range(2)]
    otm = [E_(nc.sbuf_tensor(f"otm{i}", [128, K], BF16)) for i in range(2)]
    ofm = [E_(nc.sbuf_tensor(f"ofm{i}", [K, 128], BF16)) for i in range(2)]
    sA = E_(nc.sbuf_tensor("sA", [K, K + 1], BF16))
    sL = E_(nc.sbuf_tensor("sL", [K, K + 1], F32))
    g8 = E_(nc.sbuf_tensor("g8", [K, NE * (K + 1)], F32))
    sini = E_(nc.sbuf_tensor("sini", [K, K + 1], F32))
    sini_b = E_(nc.sbuf_tensor("sini_b", [K, K + 1], BF16))
    stat = E_(nc.sbuf_tensor("stat", [128, 16], F32))
    lg = E_(nc.sbuf_tensor("lg", [128, NE], F32))
    mx8 = E_(nc.sbuf_tensor("mx8", [128, NE], F32))
    e8 = E_(nc.sbuf_tensor("e8", [128, NE], F32))
    sel8 = E_(nc.sbuf_tensor("sel8", [128, NE], F32))
    gu8 = E_(nc.sbuf_tensor("gu8", [128, NE], F32))
    gts = E_(nc.sbuf_tensor("gts", [128, NE], BF16))

    PS = [E_(nc.psum_tensor(f"PS{i}", [128, 512], F32)) for i in range(8)]
    PSB = [p.ap().bitcast(BF16) for p in PS]

    SEM = {}
    for k in ("t", "v", "s", "g", "y", "cc"):
        SEM[k] = E_(nc.semaphore("sem_" + k))
    block = E_(nc.Block())

    Q = {k: [] for k in ("t", "v", "s", "g", "y")}
    C = {k: 0 for k in ("t", "v", "s", "g", "y", "cc")}
    WAITED = {k: {} for k in ("t", "v", "s", "g", "y")}

    def op(eng, f, inc=0, key=None):
        if inc:
            key = key or eng
            C[key] += inc
            def g_(E, f=f, key=key, inc=inc):
                f(E).then_inc(SEM[key], inc)
            Q[eng].append(g_)
            return C[key]
        Q[eng].append(f)
        return None

    def wt(eng, key, val):
        if not val:
            return
        assert val <= C[key], f"forward wait {eng} on {key}: {val} > {C[key]}"
        if WAITED[eng].get(key, -1) >= val:
            return
        WAITED[eng][key] = val
        Q[eng].append(lambda E, key=key, val=val: E.wait_ge(SEM[key], val))

    n_x = op("y", lambda E: E.dma_start(
        out=res[:, :].rearrange("p (tt d) -> p tt d", tt=NT),
        in_=x_d.rearrange("(tt p) d -> p tt d", p=128)), inc=16)

    def dma_w1(e, g):
        return op("g", lambda E, e=e, g=g: E.dma_start(
            out=w1b[:, :].rearrange("p (c f) -> p c f", c=DC),
            in_=w1_d[e, :, g * 1024:(g + 1) * 1024].rearrange("(c p) f -> p c f", p=128)), inc=16)

    def dma_w2(e, g):
        return op("g", lambda E, e=e, g=g: E.dma_start(
            out=w2b[:, :].rearrange("p (c d) -> p c d", c=FCG),
            in_=w2_d[e, g * 1024:(g + 1) * 1024, :].rearrange("(c p) d -> p c d", p=128)), inc=16)

    n_w1 = dma_w1(0, 0)
    n_w2 = dma_w2(0, 0)
    op("g", lambda E: E.dma_start(out=wqT[:, :].rearrange("p (c k) -> p c k", c=DC),
                                  in_=wqt_d.rearrange("(c p) k -> p c k", p=128)), inc=16)
    op("g", lambda E: E.dma_start(out=wkT[:, :].rearrange("p (c k) -> p c k", c=DC),
                                  in_=wkt_d.rearrange("(c p) k -> p c k", p=128)), inc=16)
    op("g", lambda E: E.dma_start(out=wvT[:, :].rearrange("p (c k) -> p c k", c=DC),
                                  in_=wvt_d.rearrange("(c p) k -> p c k", p=128)), inc=16)
    op("g", lambda E: E.dma_start(out=woT[:, :], in_=wot_d), inc=16)
    op("g", lambda E: E.dma_start(out=rw_s[:, :].rearrange("p (c r) -> p c r", c=DC),
                                  in_=rw_d.rearrange("(c p) r -> p c r", p=128)), inc=16)
    op("g", lambda E: E.dma_start(out=rbbc[:, :], in_=rbbc_d), inc=16)
    op("g", lambda E: E.dma_start(out=b1s[:, :].rearrange("p (e f) -> p e f", e=NE),
                                  in_=b1r_d.rearrange("e p f -> p e f")), inc=16)
    op("g", lambda E: E.dma_start(out=b2s[:, :], in_=b2_d), inc=16)
    op("g", lambda E: E.dma_start(out=ident[:, :], in_=ident_d), inc=16)
    op("g", lambda E: E.dma_start(out=gsel[:, :], in_=gsel_d), inc=16)
    op("g", lambda E: E.dma_start(out=masku[:, :], in_=masku_d), inc=16)
    n_init_dma = op("g", lambda E: E.dma_start(out=pmask[:, :], in_=pmask_d), inc=16)

    op("v", lambda E: E.memset(vtm[:, :], 1.0))

    wt("t", "g", n_init_dma)
    for b in range(8):
        op("t", lambda E, b=b: E.transpose(PSB[b][:, 0:128], ident[:, :],
                                           ident[:, :]), inc=1)
    n_warm = C["t"]
    wt("v", "t", n_warm)
    nwv = op("v", lambda E: E.tensor_copy(stat[:, 8:9], PSB[0][0:128, 0:1]), inc=1)
    wt("t", "v", nwv)
    wt("s", "t", n_warm)

    tr_copy_done = {}
    state = dict(prev_nxh=0, prev_tr_t=0, nex_prev=0)

    def emit_ln(i):
        sl = res[:, i * D:(i + 1) * D]
        wt("v", "y", n_x)
        wt("s", "y", n_x)
        wt("s", "v", state.get("prev_stat_rd", 0))
        op("s", lambda E, sl=sl: E.activation(PS[7][:, :], sl[:, 0:512], AF.Square,
                                              accum_out=stat[:, 14:15]))
        op("s", lambda E, sl=sl: E.activation(PS[7][:, :], sl[:, 512:1024], AF.Square,
                                              accum_out=stat[:, 1:2]))
        nsq = op("s", lambda E: E.copy(stat[:, 6:7], stat[:, 1:2]), inc=1)
        n2 = op("v", lambda E, sl=sl: E.reduce_sum(stat[:, 0:1], sl, axis=AX.X), inc=1)
        wt("v", "v", n2)
        wt("v", "s", nsq)
        op("v", lambda E: E.tensor_scalar(stat[:, 2:3], stat[:, 0:1], 1.0 / D, None, ALU.mult))
        ng1 = op("v", lambda E: E.scalar_tensor_tensor(stat[:, 15:16], stat[:, 1:2], 1.0,
                                                       stat[:, 14:15], ALU.mult, ALU.add), inc=1)
        state["prev_stat_rd"] = ng1
        wt("v", "v", ng1)
        op("v", lambda E: E.scalar_tensor_tensor(stat[:, 4:5], stat[:, 2:3], 1.0,
                                                 stat[:, 2:3], ALU.mult, ALU.mult))
        ng2 = op("v", lambda E: E.tensor_scalar(stat[:, 3:4], stat[:, 15:16], 1.0 / D, EPS_LN,
                                                ALU.mult, ALU.add), inc=1)
        wt("v", "v", ng2)
        nv = op("v", lambda E: E.tensor_tensor(stat[:, 5:6], stat[:, 3:4], stat[:, 4:5],
                                               ALU.subtract), inc=1)
        wt("s", "v", nv)
        nln = op("s", lambda E: E.activation(stat[:, 6:7], stat[:, 5:6], AF.Ln), inc=1)
        wt("s", "s", nln)
        op("s", lambda E: E.activation(stat[:, 7:8], stat[:, 6:7], AF.Exp, scale=-0.5))
        ns = op("s", lambda E: E.copy(stat[:, 6:7], stat[:, 7:8]), inc=1)
        wt("v", "s", ns)
        wt("v", "t", state["prev_tr_t"])
        op("v", lambda E, sl=sl: E.tensor_scalar(
            xh[:, :], sl, stat[:, 2:3], stat[:, 7:8], ALU.subtract, ALU.mult))
        nxh = op("v", lambda E: E.tensor_copy(stat[:, 8:9], stat[:, 7:8]), inc=1)
        state["prev_nxh"] = nxh
        return nxh

    def emit_transposes(i, nxh, banks=(3, 4)):
        wt("t", "v", nxh)
        wt("t", "g", n_init_dma)
        last_t = 0
        for cp in range(DC // 2):
            b = banks[cp % 2]
            key = ("tr", b)
            wt("t", "v", tr_copy_done.get(key, 0))
            for half in range(2):
                c = cp * 2 + half
                ntr = op("t", lambda E, c=c, b=b, half=half: E.transpose(
                    PSB[b][:, half * 128:(half + 1) * 128],
                    xh[:, c * 128:(c + 1) * 128], ident[:, :]),
                    inc=(1 if half == 1 else 0))
            ntr = C["t"]
            wt("v", "t", ntr)
            c0 = cp * 2
            ncp = op("v", lambda E, c0=c0, b=b, i=i: E.tensor_copy(
                xfm[:, :].rearrange("p (c t) -> p c t", c=DC)[:, c0:c0 + 2,
                                                             i * 128:(i + 1) * 128],
                PSB[b][:, 0:256].rearrange("p (c t) -> p c t", c=2)), inc=1)
            tr_copy_done[key] = ncp
            last_t = ntr
        state["prev_tr_t"] = last_t
        return C["v"]

    kv_ready = [0] * NT
    ktr_copy = [0] * NT
    qk_read = {}
    nfm_last = 0

    for i in range(NT):
        nxh = emit_ln(i)
        if DEBUG and i == 0:
            wt("g", "v", nxh)
            op("g", lambda E: E.dma_start(out=dbg_stat, in_=stat[:, :]), inc=16)
            op("g", lambda E: E.dma_start(out=dbg_xh, in_=xh[:, :]), inc=16)
            op("y", lambda E: E.dma_start(
                out=dbg_res0.rearrange("(tt p) d -> p tt d", p=128),
                in_=res[:, :].rearrange("p (tt d) -> p tt d", tt=NT)), inc=16, key="y")
        n_xfm_i = emit_transposes(i, nxh)

        wt("t", "g", n_init_dma)
        wt("t", "v", n_xfm_i)
        wt("t", "v", qk_read.get("vps", 0))
        for c in range(DC):
            op("t", lambda E, c=c, i=i: E.matmul(
                PS[2][0:128, 0:K],
                xfm[:, c * T + i * 128: c * T + i * 128 + 128],
                wvT[:, c * K:(c + 1) * K],
                start=(c == 0), stop=(c == DC - 1)),
                inc=(1 if c == DC - 1 else 0))
        n_vmm = C["t"]
        wt("v", "t", n_vmm)
        nvc = op("v", lambda E, i=i: E.tensor_copy(
            vtm[:, i * (K + 1): i * (K + 1) + K], PS[2][0:128, 0:K]), inc=1)
        qk_read["vps"] = nvc

        if i % 4 == 3:
            tq = i // 4
            for (psb, wT, dstfm) in ((0, wqT, qfm), (1, wkT, kfm)):
                keyq = ("qkps", psb)
                wt("t", "v", n_xfm_i)
                wt("t", "v", qk_read.get(keyq, 0))
                for c in range(DC):
                    op("t", lambda E, c=c, tq=tq, psb=psb, wT=wT: E.matmul(
                        PS[psb][0:K, :],
                        wT[:, c * K:(c + 1) * K],
                        xfm[:, c * T + tq * 512: c * T + (tq + 1) * 512],
                        start=(c == 0), stop=(c == DC - 1)),
                        inc=(1 if c == DC - 1 else 0))
                nqk = C["t"]
                wt("v", "t", nqk)
                op("v", lambda E, psb=psb: E.tensor_scalar(
                    eluA[:, :], PS[psb][0:K, :], 0.0, None, ALU.min))
                nrd = op("v", lambda E, psb=psb: E.tensor_scalar(
                    eluB[:, :], PS[psb][0:K, :], 0.0, None, ALU.max), inc=1)
                qk_read[keyq] = nrd
                wt("s", "v", nrd)
                nex = op("s", lambda E: E.activation(eluA[:, :], eluA[:, :], AF.Exp), inc=1)
                wt("v", "s", nex)
                wt("v", "v", nrd)
                nfm_last = op("v", lambda E, dstfm=dstfm, tq=tq: E.tensor_tensor(
                    dstfm[:, tq * 512:(tq + 1) * 512], eluA[:, :], eluB[:, :], ALU.add), inc=1)
            for ii in range(tq * 4, tq * 4 + 4):
                wt("t", "v", nfm_last)
                wt("t", "v", ktr_copy[ii - 1] if ii > 0 else 0)
                nkt = op("t", lambda E, ii=ii: E.transpose(
                    PSB[5][0:128, 0:K], kfm[:, ii * 128:(ii + 1) * 128],
                    ident[0:K, 0:K]), inc=1)
                wt("v", "t", nkt)
                nkc = op("v", lambda E, ii=ii: E.tensor_copy(
                    ktm[:, ii * K:(ii + 1) * K], PSB[5][0:128, 0:K]), inc=1)
                ktr_copy[ii] = nkc
                kv_ready[ii] = nkc
            for ii in range(tq * 4, tq * 4 + 4):
                wt("t", "v", kv_ready[ii])
                op("t", lambda E, ii=ii: E.matmul(
                    PS[6][0:K, 0:K + 1],
                    ktm[:, ii * K:(ii + 1) * K],
                    vtm[:, ii * (K + 1):(ii + 1) * (K + 1)],
                    start=(ii == 0), stop=(ii == NT - 1)),
                    inc=(1 if ii == NT - 1 else 0))
    n_sloc = C["t"]

    if DEBUG:
        wt("g", "v", C["v"])
        wt("g", "s", C["s"])
        op("g", lambda E: E.dma_start(out=dbg_xfm, in_=xfm[:, :]), inc=16)
        op("g", lambda E: E.dma_start(out=dbg_q, in_=qfm[:, :]), inc=16)
        op("g", lambda E: E.dma_start(out=dbg_k, in_=kfm[:, :]), inc=16)
        op("g", lambda E: E.dma_start(out=dbg_vtm, in_=vtm[:, :]), inc=16)
        op("g", lambda E: E.dma_start(out=dbg_ktm, in_=ktm[:, :]), inc=16)
    wt("v", "t", n_sloc)
    n_sl = op("v", lambda E: E.tensor_copy(sL[:, :], PS[6][0:K, 0:K + 1]), inc=1)
    wt("g", "v", n_sl)
    if DEBUG:
        op("g", lambda E: E.dma_start(out=dbg_sl, in_=sL[:, :]), inc=16)
    n_st = op("g", lambda E: E.dma_start(out=cc_in.ap(), in_=sL[:, :]), inc=16)
    wt("g", "g", n_st)
    n_cc = op("g", lambda E: E.collective_compute(
        "AllGather", ALU.bypass, replica_groups=[list(range(N_CORES))],
        ins=[cc_in.ap().opt()], outs=[cc_out.ap().opt()]), inc=1, key="cc")
    wt("g", "cc", n_cc)
    n_g8 = op("g", lambda E: E.dma_start(
        out=g8[:, :].rearrange("k (r m) -> k r m", r=NE),
        in_=cc_out.ap().rearrange("(r k) m -> k r m", k=K)), inc=16)
    wt("v", "g", n_g8)
    nacc = op("v", lambda E: E.memset(sini[:, :], 0.0), inc=1)
    for r in range(NE):
        wt("v", "v", nacc)
        nacc = op("v", lambda E, r=r: E.scalar_tensor_tensor(
            sini[:, :], g8[:, r * (K + 1):(r + 1) * (K + 1)], pmask[:, r:r + 1],
            sini[:, :], ALU.mult, ALU.add), inc=1)
    wt("v", "v", nacc)
    n_sini = op("v", lambda E: E.tensor_copy(sini_b[:, :], sini[:, :]), inc=1)
    if DEBUG:
        wt("g", "v", n_sini)
        op("g", lambda E: E.dma_start(out=dbg_sini, in_=sini[:, :]), inc=16)

    wt("t", "v", n_sini)
    wt("t", "v", C["v"])
    wt("t", "s", C["s"])
    nupd = op("t", lambda E: E.matmul(PS[6][0:K, 0:K + 1], ident[0:K, 0:K],
                                      sini_b[:, :], start=True, stop=False), inc=1)
    wt("v", "t", nupd)
    ncopy = op("v", lambda E: E.tensor_copy(
        sAll[:, 0:K + 1], PS[6][0:K, 0:K + 1]), inc=1)
    for j in range(1, NT):
        wt("t", "v", ncopy)
        nupd = op("t", lambda E, j=j: E.matmul(
            PS[6][0:K, 0:K + 1], ktm[:, (j - 1) * K:j * K],
            vtm[:, (j - 1) * (K + 1):j * (K + 1)], start=False, stop=False), inc=1)
        wt("v", "t", nupd)
        ncopy = op("v", lambda E, j=j: E.tensor_copy(
            sAll[:, j * (K + 1):(j + 1) * (K + 1)], PS[6][0:K, 0:K + 1]), inc=1)
    n_states = ncopy
    bm_done = [0, 0]
    bm_read = [0, 0]
    numused = [0, 0]
    scorein = [0, 0]
    ot_copy = [0, 0]
    ofm_read = [0, 0]
    otm_read = [0, 0]
    ao_add = [0, 0]
    res_attn_done = [0] * NT

    for j in range(NT):
        sb = j % 2
        wt("t", "v", scorein[sb])
        nB = op("t", lambda E, j=j, sb=sb: E.matmul(
            PS[sb][:, 0:128], kfm[:, j * 128:(j + 1) * 128],
            qfm[:, j * 128:(j + 1) * 128], start=True, stop=True), inc=1)
        wt("v", "t", nB)
        wt("v", "t", bm_read[sb])
        nbm = op("v", lambda E, sb=sb: E.tensor_tensor(
            bm[sb][:, :], PS[sb][:, 0:128], masku[:, :], ALU.mult), inc=1)
        bm_done[sb] = nbm
        scorein[sb] = nbm
        wt("t", "v", n_states)
        wt("t", "v", numused[sb])
        op("t", lambda E, j=j, sb=sb: E.matmul(
            PS[2 + sb][:, 0:K + 1], qfm[:, j * 128:(j + 1) * 128],
            sAll[:, j * (K + 1):(j + 1) * (K + 1)], start=True, stop=False))
        wt("t", "v", bm_done[sb])
        nnum = op("t", lambda E, j=j, sb=sb: E.matmul(
            PS[2 + sb][:, 0:K + 1], bm[sb][:, :],
            vtm[:, j * (K + 1):(j + 1) * (K + 1)], start=False, stop=True), inc=1)
        bm_read[sb] = nnum
        wt("v", "t", nnum)
        nd1 = op("v", lambda E, sb=sb: E.tensor_scalar(
            stat[:, 9:10], PS[2 + sb][:, K:K + 1], EPS_DEN, None, ALU.add), inc=1)
        wt("v", "v", nd1)
        nd2 = op("v", lambda E: E.reciprocal(stat[:, 10:11], stat[:, 9:10]), inc=1)
        wt("v", "v", nd2)
        wt("v", "t", otm_read[sb])
        not_ = op("v", lambda E, sb=sb: E.tensor_scalar(
            otm[sb][:, :], PS[2 + sb][:, 0:K], stat[:, 10:11], None, ALU.mult), inc=1)
        numused[sb] = not_
        wt("t", "v", not_)
        wt("t", "v", ot_copy[sb])
        ntr = op("t", lambda E, sb=sb: E.transpose(
            PSB[4][0:K, sb * 128:(sb + 1) * 128], otm[sb][:, :], ident[:, :]), inc=1)
        otm_read[sb] = ntr
        wt("v", "t", ntr)
        wt("v", "t", ofm_read[sb])
        ncp = op("v", lambda E, sb=sb: E.tensor_copy(
            ofm[sb][:, :], PSB[4][0:K, sb * 128:(sb + 1) * 128]), inc=1)
        ot_copy[sb] = ncp
        wt("t", "v", ncp)
        for half, bank in ((0, 5), (1, 7)):
            wt("t", "v", ao_add[half])
            nao = op("t", lambda E, sb=sb, half=half, bank=bank: E.matmul(
                PS[bank][:, :], ofm[sb][:, :], woT[:, half * 512:(half + 1) * 512],
                start=True, stop=True), inc=1)
            ofm_read[sb] = nao
            wt("v", "t", nao)
            nadd = op("v", lambda E, j=j, half=half, bank=bank: E.tensor_tensor(
                res[:, j * D + half * 512: j * D + (half + 1) * 512],
                PS[bank][:, :],
                res[:, j * D + half * 512: j * D + (half + 1) * 512], ALU.add), inc=1)
            ao_add[half] = nadd
        res_attn_done[j] = ao_add[1]

    n_dbga = 0
    if DEBUG:
        wt("y", "v", C["v"])
        n_dbga = op("y", lambda E: E.dma_start(
            out=dbg_resa.rearrange("(tt p) d -> p tt d", p=128),
            in_=res[:, :].rearrange("p (tt d) -> p tt d", tt=NT)), inc=16, key="y")

    wt("t", "v", C["v"])
    wt("t", "s", C["s"])
    wt("s", "t", C["t"])
    wt("s", "v", C["v"])
    gts_read = 0
    lg_read = 0
    gt_copy = 0
    gates_done = 0
    for i in range(NT):
        nxh = emit_ln(i)
        n_xfm_i = emit_transposes(i, nxh, banks=(3, 4))
        wt("t", "v", n_xfm_i)
        wt("t", "v", lg_read)
        for c in range(DC):
            op("t", lambda E, c=c, i=i: E.matmul(
                PS[2][:, 0:NE],
                xfm[:, c * T + i * 128: c * T + i * 128 + 128],
                rw_s[:, c * NE:(c + 1) * NE],
                start=(c == 0), stop=(c == DC - 1)),
                inc=(1 if c == DC - 1 else 0))
        nlg = C["t"]
        wt("v", "t", nlg)
        wt("v", "s", state["nex_prev"])
        lg_read = op("v", lambda E: E.tensor_tensor(lg[:, :], PS[2][:, 0:NE],
                                                    rbbc[:, :], ALU.add), inc=1)
        wt("v", "v", lg_read)
        nmx = op("v", lambda E: E.max(out=mx8[:, :], in_=lg[:, :]), inc=1)
        wt("v", "v", nmx)
        op("v", lambda E: E.tensor_scalar(stat[:, 11:12], mx8[:, 0:1], -1.0, None, ALU.mult))
        nsl = op("v", lambda E: E.tensor_scalar(sel8[:, :], lg[:, :], mx8[:, 1:2],
                                                None, ALU.is_ge), inc=1)
        wt("s", "v", nsl)
        op("s", lambda E: E.activation(e8[:, :], lg[:, :], AF.Exp,
                                       bias=stat[:, 11:12]))
        nex = op("s", lambda E: E.copy(scr[:, 0:1], e8[:, 0:1]), inc=1)
        state["nex_prev"] = nex
        wt("v", "s", nex)
        wt("v", "v", nsl)
        ngu = op("v", lambda E: E.tensor_tensor(gu8[:, :], e8[:, :], sel8[:, :], ALU.mult), inc=1)
        wt("v", "v", ngu)
        ngs = op("v", lambda E: E.reduce_sum(stat[:, 12:13], gu8[:, :], axis=AX.X), inc=1)
        wt("v", "v", ngs)
        nrc = op("v", lambda E: E.reciprocal(stat[:, 13:14], stat[:, 12:13]), inc=1)
        wt("v", "v", nrc)
        wt("v", "t", gts_read)
        wt("v", "g", state.get("dbg_g", 0))
        ngt = op("v", lambda E: E.tensor_scalar(gts[:, :], gu8[:, :], stat[:, 13:14],
                                                None, ALU.mult), inc=1)
        if DEBUG:
            wt("g", "v", ngt)
            state["dbg_g"] = op("g", lambda E, i=i: E.dma_start(
                out=dbg_gates[:, i * NE:(i + 1) * NE], in_=gts[:, :]), inc=16)
        wt("t", "v", ngt)
        wt("t", "v", gt_copy)
        ngtr = op("t", lambda E: E.transpose(PSB[5][0:NE, 0:128], gts[:, :],
                                             ident[:, :]), inc=1)
        gts_read = ngtr
        wt("v", "t", ngtr)
        gt_copy = op("v", lambda E, i=i: E.tensor_copy(gtT[:, i * 128:(i + 1) * 128],
                                                       PSB[5][0:NE, 0:128]), inc=1)
        gates_done = gt_copy

    wt("t", "v", C["v"])
    wt("t", "s", C["s"])
    wt("t", "g", n_w1)
    if DEBUG:
        wt("v", "y", n_dbga)
    for wb_ in range(24):
        op("t", lambda E, wb_=wb_: E.matmul(
            PS[7][:, :], w1b[:, (wb_ % 8) * 128:(wb_ % 8) * 128 + 128],
            xfm[:, (wb_ % 4) * 512:(wb_ % 4) * 512 + 512],
            start=True, stop=True), inc=(1 if wb_ == 23 else 0))
    hps_free = {0: 0, 1: 0}
    gm_tq = [0, 0, 0, 0]
    hscr_free = [0, 0, 0, 0]
    out_free = {(0, 0): 0, (0, 1): 0, (1, 0): 0, (1, 1): 0}
    gbc_copy = 0
    last_gm = 0
    w1_ready = n_w1
    w2_ready = n_w2

    for e in range(NE):
        wt("t", "v", gates_done)
        for half in range(4):
            wt("t", "v", gbc_copy)
            ngb = op("t", lambda E, e=e, half=half: E.matmul(
                PS[6][:, :], gsel[:, e * 128:(e + 1) * 128],
                gtT[:, half * 512:(half + 1) * 512], start=True, stop=True), inc=1)
            wt("v", "t", ngb)
            wt("v", "v", last_gm)
            gbc_copy = op("v", lambda E, half=half: E.tensor_copy(
                gbc[:, half * 512:(half + 1) * 512], PS[6][:, :]), inc=1)
        for g in range(NG):
            wt("t", "g", w1_ready)
            kidx = 0
            for fc in range(FCG):
                for tq in range(NTQ):
                    hb = kidx % 2
                    wt("t", "s", hps_free[hb])
                    for c in range(DC):
                        op("t", lambda E, c=c, fc=fc, tq=tq, hb=hb: E.matmul(
                            PS[hb][:, :],
                            w1b[:, c * 1024 + fc * 128: c * 1024 + fc * 128 + 128],
                            xfm[:, c * T + tq * 512: c * T + (tq + 1) * 512],
                            start=(c == 0), stop=(c == DC - 1)),
                            inc=(1 if c == DC - 1 else 0))
                    nmm = C["t"]
                    sc = kidx % 4
                    wt("s", "t", nmm)
                    wt("s", "v", hscr_free[sc])
                    ngl = op("s", lambda E, hb=hb, e=e, g=g, fc=fc, sc=sc: E.activation(
                        hscr[sc][:, :], PS[hb][:, :], AF.Gelu,
                        bias=b1s[:, e * 32 + g * FCG + fc: e * 32 + g * FCG + fc + 1]), inc=1)
                    hps_free[hb] = ngl
                    wt("v", "s", ngl)
                    wt("v", "v", gbc_copy)
                    ngm = op("v", lambda E, fc=fc, tq=tq, sc=sc: E.tensor_tensor(
                        h[:, fc * T + tq * 512: fc * T + (tq + 1) * 512],
                        hscr[sc][:, :], gbc[:, tq * 512:(tq + 1) * 512], ALU.mult), inc=1)
                    hscr_free[sc] = ngm
                    if fc == FCG - 1:
                        gm_tq[tq] = ngm
                    kidx += 1
            h_written = C["v"]
            last_gm = h_written
            mm1_last = C["t"]
            nxt = (e, g + 1) if g + 1 < NG else ((e + 1, 0) if e + 1 < NE else None)
            if nxt is not None:
                wt("g", "t", mm1_last)
                w1_ready = dma_w1(*nxt)
            wt("t", "g", w2_ready)
            first = (e == 0 and g == 0)
            for s16 in range(NT):
                wt("t", "v", gm_tq[s16 // 4])
                slot = s16 % 2
                for dh in range(2):
                    bank = 2 + slot * 2 + dh
                    wt("t", "v", out_free[(slot, dh)])
                    if first:
                        op("t", lambda E, s16=s16, dh=dh, bank=bank: E.matmul(
                            PS[bank][:, :], gtT[:, s16 * 128:(s16 + 1) * 128],
                            b2s[:, dh * 512:(dh + 1) * 512], start=True, stop=False))
                    for fc in range(FCG):
                        op("t", lambda E, s16=s16, dh=dh, fc=fc, bank=bank, first=first: E.matmul(
                            PS[bank][:, :],
                            h[:, fc * T + s16 * 128: fc * T + s16 * 128 + 128],
                            w2b[:, fc * 1024 + dh * 512: fc * 1024 + dh * 512 + 512],
                            start=(fc == 0 and not first), stop=(fc == FCG - 1)),
                            inc=(1 if fc == FCG - 1 else 0))
                    nmm2 = C["t"]
                    wt("v", "t", nmm2)
                    nev = op("v", lambda E, s16=s16, dh=dh, bank=bank: E.tensor_tensor(
                        res[:, s16 * D + dh * 512: s16 * D + (dh + 1) * 512],
                        PS[bank][:, :],
                        res[:, s16 * D + dh * 512: s16 * D + (dh + 1) * 512],
                        ALU.add), inc=1)
                    out_free[(slot, dh)] = nev
                if e == NE - 1 and g == NG - 1:
                    wt("y", "v", out_free[(slot, 1)])
                    op("y", lambda E, s16=s16: E.dma_start(
                        out=out_d[s16 * 128:(s16 + 1) * 128, :],
                        in_=res[:, s16 * D:(s16 + 1) * D]), inc=16)
            mm2_last = C["t"]
            if nxt is not None:
                wt("g", "t", mm2_last)
                w2_ready = dma_w2(*nxt)

    wt("y", "y", C["y"])
    if DEBUG:
        wt("g", "g", C["g"])

    @block.gpsimd
    def _(En):
        for f in Q["g"]:
            f(En)

    @block.tensor
    def _(En):
        for f in Q["t"]:
            f(En)

    @block.vector
    def _(En):
        for f in Q["v"]:
            f(En)

    @block.scalar
    def _(En):
        for f in Q["s"]:
            f(En)

    @block.sync
    def _(En):
        for f in Q["y"]:
            f(En)

    ctx.close()
    return nc


_NC_CACHE = None


def _get_nc():
    global _NC_CACHE
    if _NC_CACHE is None:
        _NC_CACHE = build_program()
    return _NC_CACHE


def kernel(L0_attn_x, L0_attn_Wq, L0_attn_Wk, L0_attn_Wv, L0_attn_Wo,
           L0_ffn_router_w, L0_ffn_router_b, L0_ffn_w1, L0_ffn_b1,
           L0_ffn_w2, L0_ffn_b2):
    x = np.ascontiguousarray(np.asarray(L0_attn_x, dtype=np.float32).reshape(16384, 1024))
    wq = np.asarray(L0_attn_Wq, dtype=np.float32)
    wk = np.asarray(L0_attn_Wk, dtype=np.float32)
    wv = np.asarray(L0_attn_Wv, dtype=np.float32)
    wo = np.asarray(L0_attn_Wo, dtype=np.float32)
    rw = np.ascontiguousarray(np.asarray(L0_ffn_router_w, dtype=np.float32))
    rb = np.asarray(L0_ffn_router_b, dtype=np.float32)
    w1 = np.ascontiguousarray(np.asarray(L0_ffn_w1, dtype=np.float32))
    b1 = np.asarray(L0_ffn_b1, dtype=np.float32)
    w2 = np.ascontiguousarray(np.asarray(L0_ffn_w2, dtype=np.float32))
    b2 = np.ascontiguousarray(np.asarray(L0_ffn_b2, dtype=np.float32))

    wqt = np.ascontiguousarray(wq.T)
    wkt = np.ascontiguousarray(wk.T)
    wvt = np.ascontiguousarray(wv.T)
    wot = np.ascontiguousarray(wo.T)
    rbbc = np.ascontiguousarray(np.broadcast_to(rb[None, :], (128, NE)))
    b1r = np.ascontiguousarray(b1.reshape(NE, 32, 128).transpose(0, 2, 1))
    ident = np.eye(128, dtype=ml_dtypes.bfloat16)
    gsel = np.zeros((NE, NE * 128), dtype=ml_dtypes.bfloat16)
    for e in range(NE):
        gsel[e, e * 128:(e + 1) * 128] = ml_dtypes.bfloat16(1.0)
    masku = np.ascontiguousarray(np.triu(np.ones((128, 128), dtype=np.float32)))

    nc = _get_nc()
    in_maps = []
    for c in range(N_CORES):
        pm = np.zeros((K, NE), dtype=np.float32)
        pm[:, :c] = 1.0
        in_maps.append(dict(
            x=np.ascontiguousarray(x[c * T:(c + 1) * T]),
            wqt=wqt, wkt=wkt, wvt=wvt, wot=wot,
            rw=rw, rbbc=rbbc, w1=w1, b1r=b1r, w2=w2, b2=b2,
            ident=ident, gsel=gsel, masku=masku, pmask=pm,
        ))
    res = run_bass_kernel_spmd(nc, in_maps, core_ids=list(range(N_CORES)))
    if DEBUG:
        kernel._last = res
    out = np.concatenate([res.results[c]["out"] for c in range(N_CORES)], axis=0)
    return out.reshape(1, 16384, 1024)


# version 16 (final)
# speedup vs baseline: 1.0382x; 1.0084x over previous
import numpy as np
import ml_dtypes
import concourse.bass as bass
import concourse.mybir as mybir
from concourse.bass_utils import run_bass_kernel_spmd
from contextlib import ExitStack

F32 = mybir.dt.float32
BF16 = mybir.dt.bfloat16
AF = mybir.ActivationFunctionType
ALU = mybir.AluOpType
AX = mybir.AxisListType

D = 1024
DC = 8
K = 64
T = 2048
NT = 16
NTQ = 4
NE = 8
FF = 4096
NG = 4
FCG = 8
N_CORES = 8
EPS_LN = 1e-5
EPS_DEN = 1e-6

DEBUG = False


def build_program():
    nc = bass.Bass("TRN2", target_bir_lowering=False, debug=False)

    x_d = nc.dram_tensor("x", [T, D], F32, kind="ExternalInput").ap()
    wqt_d = nc.dram_tensor("wqt", [D, K], F32, kind="ExternalInput").ap()
    wkt_d = nc.dram_tensor("wkt", [D, K], F32, kind="ExternalInput").ap()
    wvt_d = nc.dram_tensor("wvt", [D, K], F32, kind="ExternalInput").ap()
    wot_d = nc.dram_tensor("wot", [K, D], F32, kind="ExternalInput").ap()
    rw_d = nc.dram_tensor("rw", [D, NE], F32, kind="ExternalInput").ap()
    rbbc_d = nc.dram_tensor("rbbc", [128, NE], F32, kind="ExternalInput").ap()
    w1_d = nc.dram_tensor("w1", [NE, D, FF], F32, kind="ExternalInput").ap()
    b1r_d = nc.dram_tensor("b1r", [NE, 128, 32], F32, kind="ExternalInput").ap()
    w2_d = nc.dram_tensor("w2", [NE, FF, D], F32, kind="ExternalInput").ap()
    b2_d = nc.dram_tensor("b2", [NE, D], F32, kind="ExternalInput").ap()
    ident_d = nc.dram_tensor("ident", [128, 128], BF16, kind="ExternalInput").ap()
    gsel_d = nc.dram_tensor("gsel", [NE, NE * 128], BF16, kind="ExternalInput").ap()
    masku_d = nc.dram_tensor("masku", [128, 128], F32, kind="ExternalInput").ap()
    pmask_d = nc.dram_tensor("pmask", [K, NE], F32, kind="ExternalInput").ap()
    out_d = nc.dram_tensor("out", [T, D], F32, kind="ExternalOutput").ap()
    cc_in = nc.dram_tensor("cc_in", [K, K + 1], BF16)
    cc_out = nc.dram_tensor("cc_out", [N_CORES * K, K + 1], BF16, addr_space="Shared")
    if DEBUG:
        dbg_q = nc.dram_tensor("dbg_q", [K, T], F32, kind="ExternalOutput").ap()
        dbg_k = nc.dram_tensor("dbg_k", [K, T], F32, kind="ExternalOutput").ap()
        dbg_sini = nc.dram_tensor("dbg_sini", [K, K + 1], F32, kind="ExternalOutput").ap()
        dbg_resa = nc.dram_tensor("dbg_resa", [T, D], F32, kind="ExternalOutput").ap()
        dbg_gates = nc.dram_tensor("dbg_gates", [128, NT * NE], F32, kind="ExternalOutput").ap()
        dbg_xfm = nc.dram_tensor("dbg_xfm", [128, DC * T], F32, kind="ExternalOutput").ap()
        dbg_res0 = nc.dram_tensor("dbg_res0", [T, D], F32, kind="ExternalOutput").ap()
        dbg_stat = nc.dram_tensor("dbg_stat", [128, 16], F32, kind="ExternalOutput").ap()
        dbg_xh = nc.dram_tensor("dbg_xh", [128, D], F32, kind="ExternalOutput").ap()
        dbg_sl = nc.dram_tensor("dbg_sl", [K, K + 1], F32, kind="ExternalOutput").ap()
        dbg_vtm = nc.dram_tensor("dbg_vtm", [128, NT * (K + 1)], F32, kind="ExternalOutput").ap()
        dbg_ktm = nc.dram_tensor("dbg_ktm", [128, NT * K], F32, kind="ExternalOutput").ap()

    ctx = ExitStack()
    E_ = ctx.enter_context
    res = E_(nc.sbuf_tensor("res", [128, NT * D], F32))
    xfm = E_(nc.sbuf_tensor("xfm", [128, DC * T], BF16))
    h = E_(nc.sbuf_tensor("h", [128, FCG * T], BF16))
    w1b = E_(nc.sbuf_tensor("w1b", [128, DC * 1024], BF16))
    w2b = E_(nc.sbuf_tensor("w2b", [128, FCG * 1024], BF16))
    gbc = E_(nc.sbuf_tensor("gbc", [128, T], BF16))
    hscr = [E_(nc.sbuf_tensor(f"hscr{i}", [128, 512], BF16)) for i in range(4)]
    qfm = E_(nc.sbuf_tensor("qfm", [K, T], BF16))
    kfm = E_(nc.sbuf_tensor("kfm", [K, T], BF16))
    ktm = E_(nc.sbuf_tensor("ktm", [128, NT * K], BF16))
    vtm = E_(nc.sbuf_tensor("vtm", [128, NT * (K + 1)], BF16))
    wqT = E_(nc.sbuf_tensor("wqT", [128, DC * K], BF16))
    wkT = E_(nc.sbuf_tensor("wkT", [128, DC * K], BF16))
    wvT = E_(nc.sbuf_tensor("wvT", [128, DC * K], BF16))
    woT = E_(nc.sbuf_tensor("woT", [K, D], BF16))
    rw_s = E_(nc.sbuf_tensor("rw_s", [128, DC * NE], BF16))
    rbbc = E_(nc.sbuf_tensor("rbbc_s", [128, NE], F32))
    b1s = E_(nc.sbuf_tensor("b1s", [128, NE * 32], F32))
    b2s = E_(nc.sbuf_tensor("b2s", [NE, D], BF16))
    ident = E_(nc.sbuf_tensor("ident_s", [128, 128], BF16))
    gsel = E_(nc.sbuf_tensor("gsel_s", [NE, NE * 128], BF16))
    masku = E_(nc.sbuf_tensor("masku_s", [128, 128], F32))
    pmask = E_(nc.sbuf_tensor("pmask_s", [K, NE], F32))
    gtT = E_(nc.sbuf_tensor("gtT", [NE, T], BF16))
    scr = E_(nc.sbuf_tensor("scr", [128, 16], F32))
    sAll = E_(nc.sbuf_tensor("sAll", [K, NT * (K + 1)], BF16))
    xh = E_(nc.sbuf_tensor("xh", [128, D], BF16))
    eluA = E_(nc.sbuf_tensor("eluA", [K, 512], F32))
    eluB = E_(nc.sbuf_tensor("eluB", [K, 512], F32))
    bm = [E_(nc.sbuf_tensor(f"bm{i}", [128, 128], BF16)) for i in range(2)]
    otm = [E_(nc.sbuf_tensor(f"otm{i}", [128, K], BF16)) for i in range(2)]
    ofm = [E_(nc.sbuf_tensor(f"ofm{i}", [K, 128], BF16)) for i in range(2)]
    sA = E_(nc.sbuf_tensor("sA", [K, K + 1], BF16))
    sL = E_(nc.sbuf_tensor("sL", [K, K + 1], BF16))
    g8 = E_(nc.sbuf_tensor("g8", [K, NE * (K + 1)], F32))
    sini = E_(nc.sbuf_tensor("sini", [K, K + 1], F32))
    sini_b = E_(nc.sbuf_tensor("sini_b", [K, K + 1], BF16))
    stat = E_(nc.sbuf_tensor("stat", [128, 16], F32))
    lg = E_(nc.sbuf_tensor("lg", [128, NE], F32))
    mx8 = E_(nc.sbuf_tensor("mx8", [128, NE], F32))
    e8 = E_(nc.sbuf_tensor("e8", [128, NE], F32))
    sel8 = E_(nc.sbuf_tensor("sel8", [128, NE], F32))
    gu8 = E_(nc.sbuf_tensor("gu8", [128, NE], F32))
    gts = E_(nc.sbuf_tensor("gts", [128, NE], BF16))

    PS = [E_(nc.psum_tensor(f"PS{i}", [128, 512], F32)) for i in range(8)]
    PSB = [p.ap().bitcast(BF16) for p in PS]

    SEM = {}
    for k in ("t", "v", "s", "g", "y", "cc"):
        SEM[k] = E_(nc.semaphore("sem_" + k))
    block = E_(nc.Block())

    Q = {k: [] for k in ("t", "v", "s", "g", "y")}
    C = {k: 0 for k in ("t", "v", "s", "g", "y", "cc")}
    WAITED = {k: {} for k in ("t", "v", "s", "g", "y")}

    def op(eng, f, inc=0, key=None):
        if inc:
            key = key or eng
            C[key] += inc
            def g_(E, f=f, key=key, inc=inc):
                f(E).then_inc(SEM[key], inc)
            Q[eng].append(g_)
            return C[key]
        Q[eng].append(f)
        return None

    def wt(eng, key, val):
        if not val:
            return
        assert val <= C[key], f"forward wait {eng} on {key}: {val} > {C[key]}"
        if WAITED[eng].get(key, -1) >= val:
            return
        WAITED[eng][key] = val
        Q[eng].append(lambda E, key=key, val=val: E.wait_ge(SEM[key], val))

    n_x = op("y", lambda E: E.dma_start(
        out=res[:, :].rearrange("p (tt d) -> p tt d", tt=NT),
        in_=x_d.rearrange("(tt p) d -> p tt d", p=128)), inc=16)

    def dma_w1(e, g):
        return op("g", lambda E, e=e, g=g: E.dma_start(
            out=w1b[:, :].rearrange("p (c f) -> p c f", c=DC),
            in_=w1_d[e, :, g * 1024:(g + 1) * 1024].rearrange("(c p) f -> p c f", p=128)), inc=16)

    def dma_w2(e, g):
        return op("g", lambda E, e=e, g=g: E.dma_start(
            out=w2b[:, :].rearrange("p (c d) -> p c d", c=FCG),
            in_=w2_d[e, g * 1024:(g + 1) * 1024, :].rearrange("(c p) d -> p c d", p=128)), inc=16)

    n_w1 = dma_w1(0, 0)
    n_w2 = dma_w2(0, 0)
    op("g", lambda E: E.dma_start(out=wqT[:, :].rearrange("p (c k) -> p c k", c=DC),
                                  in_=wqt_d.rearrange("(c p) k -> p c k", p=128)), inc=16)
    op("g", lambda E: E.dma_start(out=wkT[:, :].rearrange("p (c k) -> p c k", c=DC),
                                  in_=wkt_d.rearrange("(c p) k -> p c k", p=128)), inc=16)
    op("g", lambda E: E.dma_start(out=wvT[:, :].rearrange("p (c k) -> p c k", c=DC),
                                  in_=wvt_d.rearrange("(c p) k -> p c k", p=128)), inc=16)
    op("g", lambda E: E.dma_start(out=woT[:, :], in_=wot_d), inc=16)
    op("g", lambda E: E.dma_start(out=rw_s[:, :].rearrange("p (c r) -> p c r", c=DC),
                                  in_=rw_d.rearrange("(c p) r -> p c r", p=128)), inc=16)
    op("g", lambda E: E.dma_start(out=rbbc[:, :], in_=rbbc_d), inc=16)
    op("g", lambda E: E.dma_start(out=b1s[:, :].rearrange("p (e f) -> p e f", e=NE),
                                  in_=b1r_d.rearrange("e p f -> p e f")), inc=16)
    op("g", lambda E: E.dma_start(out=b2s[:, :], in_=b2_d), inc=16)
    op("g", lambda E: E.dma_start(out=ident[:, :], in_=ident_d), inc=16)
    op("g", lambda E: E.dma_start(out=gsel[:, :], in_=gsel_d), inc=16)
    op("g", lambda E: E.dma_start(out=masku[:, :], in_=masku_d), inc=16)
    n_init_dma = op("g", lambda E: E.dma_start(out=pmask[:, :], in_=pmask_d), inc=16)

    op("v", lambda E: E.memset(vtm[:, :], 1.0))

    wt("t", "g", n_init_dma)
    for b in range(8):
        op("t", lambda E, b=b: E.transpose(PSB[b][:, 0:128], ident[:, :],
                                           ident[:, :]), inc=1)
    n_warm = C["t"]
    wt("v", "t", n_warm)
    nwv = op("v", lambda E: E.tensor_copy(stat[:, 8:9], PSB[0][0:128, 0:1]), inc=1)
    wt("t", "v", nwv)
    wt("s", "t", n_warm)

    tr_copy_done = {}
    state = dict(prev_nxh=0, prev_tr_t=0, nex_prev=0)

    def emit_ln(i):
        sl = res[:, i * D:(i + 1) * D]
        wt("v", "y", n_x)
        wt("s", "y", n_x)
        wt("s", "v", state.get("prev_stat_rd", 0))
        op("s", lambda E, sl=sl: E.activation(PS[7][:, :], sl[:, 0:512], AF.Square,
                                              accum_out=stat[:, 14:15]))
        op("s", lambda E, sl=sl: E.activation(PS[7][:, :], sl[:, 512:1024], AF.Square,
                                              accum_out=stat[:, 1:2]))
        nsq = op("s", lambda E: E.copy(stat[:, 6:7], stat[:, 1:2]), inc=1)
        n2 = op("v", lambda E, sl=sl: E.reduce_sum(stat[:, 0:1], sl, axis=AX.X), inc=1)
        wt("v", "v", n2)
        wt("v", "s", nsq)
        op("v", lambda E: E.tensor_scalar(stat[:, 2:3], stat[:, 0:1], 1.0 / D, None, ALU.mult))
        ng1 = op("v", lambda E: E.scalar_tensor_tensor(stat[:, 15:16], stat[:, 1:2], 1.0,
                                                       stat[:, 14:15], ALU.mult, ALU.add), inc=1)
        state["prev_stat_rd"] = ng1
        wt("v", "v", ng1)
        op("v", lambda E: E.scalar_tensor_tensor(stat[:, 4:5], stat[:, 2:3], 1.0,
                                                 stat[:, 2:3], ALU.mult, ALU.mult))
        ng2 = op("v", lambda E: E.tensor_scalar(stat[:, 3:4], stat[:, 15:16], 1.0 / D, EPS_LN,
                                                ALU.mult, ALU.add), inc=1)
        wt("v", "v", ng2)
        nv = op("v", lambda E: E.tensor_tensor(stat[:, 5:6], stat[:, 3:4], stat[:, 4:5],
                                               ALU.subtract), inc=1)
        wt("s", "v", nv)
        nln = op("s", lambda E: E.activation(stat[:, 6:7], stat[:, 5:6], AF.Ln), inc=1)
        wt("s", "s", nln)
        op("s", lambda E: E.activation(stat[:, 7:8], stat[:, 6:7], AF.Exp, scale=-0.5))
        ns = op("s", lambda E: E.copy(stat[:, 6:7], stat[:, 7:8]), inc=1)
        wt("v", "s", ns)
        wt("v", "t", state["prev_tr_t"])
        op("v", lambda E, sl=sl: E.tensor_scalar(
            xh[:, :], sl, stat[:, 2:3], stat[:, 7:8], ALU.subtract, ALU.mult))
        nxh = op("v", lambda E: E.tensor_copy(stat[:, 8:9], stat[:, 7:8]), inc=1)
        state["prev_nxh"] = nxh
        return nxh

    def emit_transposes(i, nxh, banks=(3, 4)):
        wt("t", "v", nxh)
        wt("t", "g", n_init_dma)
        last_t = 0
        for cp in range(DC // 2):
            b = banks[cp % 2]
            key = ("tr", b)
            wt("t", "v", tr_copy_done.get(key, 0))
            for half in range(2):
                c = cp * 2 + half
                ntr = op("t", lambda E, c=c, b=b, half=half: E.transpose(
                    PSB[b][:, half * 128:(half + 1) * 128],
                    xh[:, c * 128:(c + 1) * 128], ident[:, :]),
                    inc=(1 if half == 1 else 0))
            ntr = C["t"]
            wt("v", "t", ntr)
            c0 = cp * 2
            ncp = op("v", lambda E, c0=c0, b=b, i=i: E.tensor_copy(
                xfm[:, :].rearrange("p (c t) -> p c t", c=DC)[:, c0:c0 + 2,
                                                             i * 128:(i + 1) * 128],
                PSB[b][:, 0:256].rearrange("p (c t) -> p c t", c=2)), inc=1)
            tr_copy_done[key] = ncp
            last_t = ntr
        state["prev_tr_t"] = last_t
        return C["v"]

    kv_ready = [0] * NT
    ktr_copy = [0] * NT
    qk_read = {}
    nfm_last = 0

    for i in range(NT):
        nxh = emit_ln(i)
        if DEBUG and i == 0:
            wt("g", "v", nxh)
            op("g", lambda E: E.dma_start(out=dbg_stat, in_=stat[:, :]), inc=16)
            op("g", lambda E: E.dma_start(out=dbg_xh, in_=xh[:, :]), inc=16)
            op("y", lambda E: E.dma_start(
                out=dbg_res0.rearrange("(tt p) d -> p tt d", p=128),
                in_=res[:, :].rearrange("p (tt d) -> p tt d", tt=NT)), inc=16, key="y")
        n_xfm_i = emit_transposes(i, nxh)

        wt("t", "g", n_init_dma)
        wt("t", "v", n_xfm_i)
        wt("t", "v", qk_read.get("vps", 0))
        for c in range(DC):
            op("t", lambda E, c=c, i=i: E.matmul(
                PS[2][0:128, 0:K],
                xfm[:, c * T + i * 128: c * T + i * 128 + 128],
                wvT[:, c * K:(c + 1) * K],
                start=(c == 0), stop=(c == DC - 1)),
                inc=(1 if c == DC - 1 else 0))
        n_vmm = C["t"]
        wt("v", "t", n_vmm)
        nvc = op("v", lambda E, i=i: E.tensor_copy(
            vtm[:, i * (K + 1): i * (K + 1) + K], PS[2][0:128, 0:K]), inc=1)
        qk_read["vps"] = nvc

        if i % 4 == 3:
            tq = i // 4
            for (psb, wT, dstfm) in ((0, wqT, qfm), (1, wkT, kfm)):
                keyq = ("qkps", psb)
                wt("t", "v", n_xfm_i)
                wt("t", "v", qk_read.get(keyq, 0))
                for c in range(DC):
                    op("t", lambda E, c=c, tq=tq, psb=psb, wT=wT: E.matmul(
                        PS[psb][0:K, :],
                        wT[:, c * K:(c + 1) * K],
                        xfm[:, c * T + tq * 512: c * T + (tq + 1) * 512],
                        start=(c == 0), stop=(c == DC - 1)),
                        inc=(1 if c == DC - 1 else 0))
                nqk = C["t"]
                wt("v", "t", nqk)
                op("v", lambda E, psb=psb: E.tensor_scalar(
                    eluA[:, :], PS[psb][0:K, :], 0.0, None, ALU.min))
                nrd = op("v", lambda E, psb=psb: E.tensor_scalar(
                    eluB[:, :], PS[psb][0:K, :], 0.0, None, ALU.max), inc=1)
                qk_read[keyq] = nrd
                wt("s", "v", nrd)
                nex = op("s", lambda E: E.activation(eluA[:, :], eluA[:, :], AF.Exp), inc=1)
                wt("v", "s", nex)
                wt("v", "v", nrd)
                nfm_last = op("v", lambda E, dstfm=dstfm, tq=tq: E.tensor_tensor(
                    dstfm[:, tq * 512:(tq + 1) * 512], eluA[:, :], eluB[:, :], ALU.add), inc=1)
            for ii in range(tq * 4, tq * 4 + 4):
                wt("t", "v", nfm_last)
                wt("t", "v", ktr_copy[ii - 1] if ii > 0 else 0)
                nkt = op("t", lambda E, ii=ii: E.transpose(
                    PSB[5][0:128, 0:K], kfm[:, ii * 128:(ii + 1) * 128],
                    ident[0:K, 0:K]), inc=1)
                wt("v", "t", nkt)
                nkc = op("v", lambda E, ii=ii: E.tensor_copy(
                    ktm[:, ii * K:(ii + 1) * K], PSB[5][0:128, 0:K]), inc=1)
                ktr_copy[ii] = nkc
                kv_ready[ii] = nkc
            for ii in range(tq * 4, tq * 4 + 4):
                wt("t", "v", kv_ready[ii])
                op("t", lambda E, ii=ii: E.matmul(
                    PS[6][0:K, 0:K + 1],
                    ktm[:, ii * K:(ii + 1) * K],
                    vtm[:, ii * (K + 1):(ii + 1) * (K + 1)],
                    start=(ii == 0), stop=(ii == NT - 1)),
                    inc=(1 if ii == NT - 1 else 0))
    n_sloc = C["t"]

    if DEBUG:
        wt("g", "v", C["v"])
        wt("g", "s", C["s"])
        op("g", lambda E: E.dma_start(out=dbg_xfm, in_=xfm[:, :]), inc=16)
        op("g", lambda E: E.dma_start(out=dbg_q, in_=qfm[:, :]), inc=16)
        op("g", lambda E: E.dma_start(out=dbg_k, in_=kfm[:, :]), inc=16)
        op("g", lambda E: E.dma_start(out=dbg_vtm, in_=vtm[:, :]), inc=16)
        op("g", lambda E: E.dma_start(out=dbg_ktm, in_=ktm[:, :]), inc=16)
    wt("v", "t", n_sloc)
    n_sl = op("v", lambda E: E.tensor_copy(sL[:, :], PS[6][0:K, 0:K + 1]), inc=1)
    wt("g", "v", n_sl)
    if DEBUG:
        op("g", lambda E: E.dma_start(out=dbg_sl, in_=sL[:, :]), inc=16)
    n_st = op("g", lambda E: E.dma_start(out=cc_in.ap(), in_=sL[:, :]), inc=16)
    wt("g", "g", n_st)
    n_cc = op("g", lambda E: E.collective_compute(
        "AllGather", ALU.bypass, replica_groups=[list(range(N_CORES))],
        ins=[cc_in.ap().opt()], outs=[cc_out.ap().opt()]), inc=1, key="cc")
    wt("g", "cc", n_cc)
    n_g8 = op("g", lambda E: E.dma_start(
        out=g8[:, :].rearrange("k (r m) -> k r m", r=NE),
        in_=cc_out.ap().rearrange("(r k) m -> k r m", k=K)), inc=16)
    wt("v", "g", n_g8)
    nacc = op("v", lambda E: E.memset(sini[:, :], 0.0), inc=1)
    for r in range(NE):
        wt("v", "v", nacc)
        nacc = op("v", lambda E, r=r: E.scalar_tensor_tensor(
            sini[:, :], g8[:, r * (K + 1):(r + 1) * (K + 1)], pmask[:, r:r + 1],
            sini[:, :], ALU.mult, ALU.add), inc=1)
    wt("v", "v", nacc)
    n_sini = op("v", lambda E: E.tensor_copy(sini_b[:, :], sini[:, :]), inc=1)
    if DEBUG:
        wt("g", "v", n_sini)
        op("g", lambda E: E.dma_start(out=dbg_sini, in_=sini[:, :]), inc=16)

    wt("t", "v", n_sini)
    wt("t", "v", C["v"])
    wt("t", "s", C["s"])
    nupd = op("t", lambda E: E.matmul(PS[6][0:K, 0:K + 1], ident[0:K, 0:K],
                                      sini_b[:, :], start=True, stop=False), inc=1)
    wt("v", "t", nupd)
    ncopy = op("v", lambda E: E.tensor_copy(
        sAll[:, 0:K + 1], PS[6][0:K, 0:K + 1]), inc=1)
    for j in range(1, NT):
        wt("t", "v", ncopy)
        nupd = op("t", lambda E, j=j: E.matmul(
            PS[6][0:K, 0:K + 1], ktm[:, (j - 1) * K:j * K],
            vtm[:, (j - 1) * (K + 1):j * (K + 1)], start=False, stop=False), inc=1)
        wt("v", "t", nupd)
        ncopy = op("v", lambda E, j=j: E.tensor_copy(
            sAll[:, j * (K + 1):(j + 1) * (K + 1)], PS[6][0:K, 0:K + 1]), inc=1)
    n_states = ncopy
    bm_done = [0, 0]
    bm_read = [0, 0]
    numused = [0, 0]
    scorein = [0, 0]
    ot_copy = [0, 0]
    ofm_read = [0, 0]
    otm_read = [0, 0]
    ao_add = [0, 0]
    res_attn_done = [0] * NT

    for j in range(NT):
        sb = j % 2
        wt("t", "v", scorein[sb])
        nB = op("t", lambda E, j=j, sb=sb: E.matmul(
            PS[sb][:, 0:128], kfm[:, j * 128:(j + 1) * 128],
            qfm[:, j * 128:(j + 1) * 128], start=True, stop=True), inc=1)
        wt("v", "t", nB)
        wt("v", "t", bm_read[sb])
        nbm = op("v", lambda E, sb=sb: E.tensor_tensor(
            bm[sb][:, :], PS[sb][:, 0:128], masku[:, :], ALU.mult), inc=1)
        bm_done[sb] = nbm
        scorein[sb] = nbm
        wt("t", "v", n_states)
        wt("t", "v", numused[sb])
        op("t", lambda E, j=j, sb=sb: E.matmul(
            PS[2 + sb][:, 0:K + 1], qfm[:, j * 128:(j + 1) * 128],
            sAll[:, j * (K + 1):(j + 1) * (K + 1)], start=True, stop=False))
        wt("t", "v", bm_done[sb])
        nnum = op("t", lambda E, j=j, sb=sb: E.matmul(
            PS[2 + sb][:, 0:K + 1], bm[sb][:, :],
            vtm[:, j * (K + 1):(j + 1) * (K + 1)], start=False, stop=True), inc=1)
        bm_read[sb] = nnum
        wt("v", "t", nnum)
        nd1 = op("v", lambda E, sb=sb: E.tensor_scalar(
            stat[:, 9:10], PS[2 + sb][:, K:K + 1], EPS_DEN, None, ALU.add), inc=1)
        wt("v", "v", nd1)
        nd2 = op("v", lambda E: E.reciprocal(stat[:, 10:11], stat[:, 9:10]), inc=1)
        wt("v", "v", nd2)
        wt("v", "t", otm_read[sb])
        not_ = op("v", lambda E, sb=sb: E.tensor_scalar(
            otm[sb][:, :], PS[2 + sb][:, 0:K], stat[:, 10:11], None, ALU.mult), inc=1)
        numused[sb] = not_
        wt("t", "v", not_)
        wt("t", "v", ot_copy[sb])
        ntr = op("t", lambda E, sb=sb: E.transpose(
            PSB[4][0:K, sb * 128:(sb + 1) * 128], otm[sb][:, :], ident[:, :]), inc=1)
        otm_read[sb] = ntr
        wt("v", "t", ntr)
        wt("v", "t", ofm_read[sb])
        ncp = op("v", lambda E, sb=sb: E.tensor_copy(
            ofm[sb][:, :], PSB[4][0:K, sb * 128:(sb + 1) * 128]), inc=1)
        ot_copy[sb] = ncp
        wt("t", "v", ncp)
        for half, bank in ((0, 5), (1, 7)):
            wt("t", "v", ao_add[half])
            nao = op("t", lambda E, sb=sb, half=half, bank=bank: E.matmul(
                PS[bank][:, :], ofm[sb][:, :], woT[:, half * 512:(half + 1) * 512],
                start=True, stop=True), inc=1)
            ofm_read[sb] = nao
            wt("v", "t", nao)
            nadd = op("v", lambda E, j=j, half=half, bank=bank: E.tensor_tensor(
                res[:, j * D + half * 512: j * D + (half + 1) * 512],
                PS[bank][:, :],
                res[:, j * D + half * 512: j * D + (half + 1) * 512], ALU.add), inc=1)
            ao_add[half] = nadd
        res_attn_done[j] = ao_add[1]

    n_dbga = 0
    if DEBUG:
        wt("y", "v", C["v"])
        n_dbga = op("y", lambda E: E.dma_start(
            out=dbg_resa.rearrange("(tt p) d -> p tt d", p=128),
            in_=res[:, :].rearrange("p (tt d) -> p tt d", tt=NT)), inc=16, key="y")

    wt("t", "v", C["v"])
    wt("t", "s", C["s"])
    wt("s", "t", C["t"])
    wt("s", "v", C["v"])
    gts_read = 0
    lg_read = 0
    gt_copy = 0
    gates_done = 0
    for i in range(NT):
        nxh = emit_ln(i)
        n_xfm_i = emit_transposes(i, nxh, banks=(3, 4))
        wt("t", "v", n_xfm_i)
        wt("t", "v", lg_read)
        for c in range(DC):
            op("t", lambda E, c=c, i=i: E.matmul(
                PS[2][:, 0:NE],
                xfm[:, c * T + i * 128: c * T + i * 128 + 128],
                rw_s[:, c * NE:(c + 1) * NE],
                start=(c == 0), stop=(c == DC - 1)),
                inc=(1 if c == DC - 1 else 0))
        nlg = C["t"]
        wt("v", "t", nlg)
        wt("v", "s", state["nex_prev"])
        lg_read = op("v", lambda E: E.tensor_tensor(lg[:, :], PS[2][:, 0:NE],
                                                    rbbc[:, :], ALU.add), inc=1)
        wt("v", "v", lg_read)
        nmx = op("v", lambda E: E.max(out=mx8[:, :], in_=lg[:, :]), inc=1)
        wt("v", "v", nmx)
        op("v", lambda E: E.tensor_scalar(stat[:, 11:12], mx8[:, 0:1], -1.0, None, ALU.mult))
        nsl = op("v", lambda E: E.tensor_scalar(sel8[:, :], lg[:, :], mx8[:, 1:2],
                                                None, ALU.is_ge), inc=1)
        wt("s", "v", nsl)
        op("s", lambda E: E.activation(e8[:, :], lg[:, :], AF.Exp,
                                       bias=stat[:, 11:12]))
        nex = op("s", lambda E: E.copy(scr[:, 0:1], e8[:, 0:1]), inc=1)
        state["nex_prev"] = nex
        wt("v", "s", nex)
        wt("v", "v", nsl)
        ngu = op("v", lambda E: E.tensor_tensor(gu8[:, :], e8[:, :], sel8[:, :], ALU.mult), inc=1)
        wt("v", "v", ngu)
        ngs = op("v", lambda E: E.reduce_sum(stat[:, 12:13], gu8[:, :], axis=AX.X), inc=1)
        wt("v", "v", ngs)
        nrc = op("v", lambda E: E.reciprocal(stat[:, 13:14], stat[:, 12:13]), inc=1)
        wt("v", "v", nrc)
        wt("v", "t", gts_read)
        wt("v", "g", state.get("dbg_g", 0))
        ngt = op("v", lambda E: E.tensor_scalar(gts[:, :], gu8[:, :], stat[:, 13:14],
                                                None, ALU.mult), inc=1)
        if DEBUG:
            wt("g", "v", ngt)
            state["dbg_g"] = op("g", lambda E, i=i: E.dma_start(
                out=dbg_gates[:, i * NE:(i + 1) * NE], in_=gts[:, :]), inc=16)
        wt("t", "v", ngt)
        wt("t", "v", gt_copy)
        ngtr = op("t", lambda E: E.transpose(PSB[5][0:NE, 0:128], gts[:, :],
                                             ident[:, :]), inc=1)
        gts_read = ngtr
        wt("v", "t", ngtr)
        gt_copy = op("v", lambda E, i=i: E.tensor_copy(gtT[:, i * 128:(i + 1) * 128],
                                                       PSB[5][0:NE, 0:128]), inc=1)
        gates_done = gt_copy

    wt("t", "v", C["v"])
    wt("t", "s", C["s"])
    wt("t", "g", n_w1)
    if DEBUG:
        wt("v", "y", n_dbga)
    for wb_ in range(24):
        op("t", lambda E, wb_=wb_: E.matmul(
            PS[7][:, :], w1b[:, (wb_ % 8) * 128:(wb_ % 8) * 128 + 128],
            xfm[:, (wb_ % 4) * 512:(wb_ % 4) * 512 + 512],
            start=True, stop=True), inc=(1 if wb_ == 23 else 0))
    hps_free = {0: 0, 1: 0}
    gm_tq = [0, 0, 0, 0]
    hscr_free = [0, 0, 0, 0]
    out_free = {(0, 0): 0, (0, 1): 0, (1, 0): 0, (1, 1): 0}
    gbc_copy = 0
    last_gm = 0
    w1_ready = n_w1
    w2_ready = n_w2

    for e in range(NE):
        wt("t", "v", gates_done)
        for half in range(4):
            wt("t", "v", gbc_copy)
            ngb = op("t", lambda E, e=e, half=half: E.matmul(
                PS[6][:, :], gsel[:, e * 128:(e + 1) * 128],
                gtT[:, half * 512:(half + 1) * 512], start=True, stop=True), inc=1)
            wt("v", "t", ngb)
            wt("v", "v", last_gm)
            gbc_copy = op("v", lambda E, half=half: E.tensor_copy(
                gbc[:, half * 512:(half + 1) * 512], PS[6][:, :]), inc=1)
        for g in range(NG):
            wt("t", "g", w1_ready)
            kidx = 0
            for fc in range(FCG):
                for tq in range(NTQ):
                    hb = kidx % 2
                    wt("t", "s", hps_free[hb])
                    for c in range(DC):
                        op("t", lambda E, c=c, fc=fc, tq=tq, hb=hb: E.matmul(
                            PS[hb][:, :],
                            w1b[:, c * 1024 + fc * 128: c * 1024 + fc * 128 + 128],
                            xfm[:, c * T + tq * 512: c * T + (tq + 1) * 512],
                            start=(c == 0), stop=(c == DC - 1)),
                            inc=(1 if c == DC - 1 else 0))
                    nmm = C["t"]
                    sc = kidx % 4
                    wt("s", "t", nmm)
                    wt("s", "v", hscr_free[sc])
                    ngl = op("s", lambda E, hb=hb, e=e, g=g, fc=fc, sc=sc: E.activation(
                        hscr[sc][:, :], PS[hb][:, :], AF.Gelu,
                        bias=b1s[:, e * 32 + g * FCG + fc: e * 32 + g * FCG + fc + 1]), inc=1)
                    hps_free[hb] = ngl
                    wt("v", "s", ngl)
                    wt("v", "v", gbc_copy)
                    ngm = op("v", lambda E, fc=fc, tq=tq, sc=sc: E.tensor_tensor(
                        h[:, fc * T + tq * 512: fc * T + (tq + 1) * 512],
                        hscr[sc][:, :], gbc[:, tq * 512:(tq + 1) * 512], ALU.mult), inc=1)
                    hscr_free[sc] = ngm
                    if fc == FCG - 1:
                        gm_tq[tq] = ngm
                    kidx += 1
            h_written = C["v"]
            last_gm = h_written
            mm1_last = C["t"]
            nxt = (e, g + 1) if g + 1 < NG else ((e + 1, 0) if e + 1 < NE else None)
            if nxt is not None:
                wt("g", "t", mm1_last)
                w1_ready = dma_w1(*nxt)
            wt("t", "g", w2_ready)
            first = (e == 0 and g == 0)
            for s16 in range(NT):
                wt("t", "v", gm_tq[s16 // 4])
                slot = s16 % 2
                for dh in range(2):
                    bank = 2 + slot * 2 + dh
                    wt("t", "v", out_free[(slot, dh)])
                    if first:
                        op("t", lambda E, s16=s16, dh=dh, bank=bank: E.matmul(
                            PS[bank][:, :], gtT[:, s16 * 128:(s16 + 1) * 128],
                            b2s[:, dh * 512:(dh + 1) * 512], start=True, stop=False))
                    for fc in range(FCG):
                        op("t", lambda E, s16=s16, dh=dh, fc=fc, bank=bank, first=first: E.matmul(
                            PS[bank][:, :],
                            h[:, fc * T + s16 * 128: fc * T + s16 * 128 + 128],
                            w2b[:, fc * 1024 + dh * 512: fc * 1024 + dh * 512 + 512],
                            start=(fc == 0 and not first), stop=(fc == FCG - 1)),
                            inc=(1 if fc == FCG - 1 else 0))
                    nmm2 = C["t"]
                    wt("v", "t", nmm2)
                    nev = op("v", lambda E, s16=s16, dh=dh, bank=bank: E.tensor_tensor(
                        res[:, s16 * D + dh * 512: s16 * D + (dh + 1) * 512],
                        PS[bank][:, :],
                        res[:, s16 * D + dh * 512: s16 * D + (dh + 1) * 512],
                        ALU.add), inc=1)
                    out_free[(slot, dh)] = nev
                if e == NE - 1 and g == NG - 1:
                    wt("y", "v", out_free[(slot, 1)])
                    op("y", lambda E, s16=s16: E.dma_start(
                        out=out_d[s16 * 128:(s16 + 1) * 128, :],
                        in_=res[:, s16 * D:(s16 + 1) * D]), inc=16)
            mm2_last = C["t"]
            if nxt is not None:
                wt("g", "t", mm2_last)
                w2_ready = dma_w2(*nxt)

    wt("y", "y", C["y"])
    if DEBUG:
        wt("g", "g", C["g"])

    @block.gpsimd
    def _(En):
        for f in Q["g"]:
            f(En)

    @block.tensor
    def _(En):
        for f in Q["t"]:
            f(En)

    @block.vector
    def _(En):
        for f in Q["v"]:
            f(En)

    @block.scalar
    def _(En):
        for f in Q["s"]:
            f(En)

    @block.sync
    def _(En):
        for f in Q["y"]:
            f(En)

    ctx.close()
    return nc


_NC_CACHE = None


def _get_nc():
    global _NC_CACHE
    if _NC_CACHE is None:
        _NC_CACHE = build_program()
    return _NC_CACHE


def kernel(L0_attn_x, L0_attn_Wq, L0_attn_Wk, L0_attn_Wv, L0_attn_Wo,
           L0_ffn_router_w, L0_ffn_router_b, L0_ffn_w1, L0_ffn_b1,
           L0_ffn_w2, L0_ffn_b2):
    x = np.ascontiguousarray(np.asarray(L0_attn_x, dtype=np.float32).reshape(16384, 1024))
    wq = np.asarray(L0_attn_Wq, dtype=np.float32)
    wk = np.asarray(L0_attn_Wk, dtype=np.float32)
    wv = np.asarray(L0_attn_Wv, dtype=np.float32)
    wo = np.asarray(L0_attn_Wo, dtype=np.float32)
    rw = np.ascontiguousarray(np.asarray(L0_ffn_router_w, dtype=np.float32))
    rb = np.asarray(L0_ffn_router_b, dtype=np.float32)
    w1 = np.ascontiguousarray(np.asarray(L0_ffn_w1, dtype=np.float32))
    b1 = np.asarray(L0_ffn_b1, dtype=np.float32)
    w2 = np.ascontiguousarray(np.asarray(L0_ffn_w2, dtype=np.float32))
    b2 = np.ascontiguousarray(np.asarray(L0_ffn_b2, dtype=np.float32))

    wqt = np.ascontiguousarray(wq.T)
    wkt = np.ascontiguousarray(wk.T)
    wvt = np.ascontiguousarray(wv.T)
    wot = np.ascontiguousarray(wo.T)
    rbbc = np.ascontiguousarray(np.broadcast_to(rb[None, :], (128, NE)))
    b1r = np.ascontiguousarray(b1.reshape(NE, 32, 128).transpose(0, 2, 1))
    ident = np.eye(128, dtype=ml_dtypes.bfloat16)
    gsel = np.zeros((NE, NE * 128), dtype=ml_dtypes.bfloat16)
    for e in range(NE):
        gsel[e, e * 128:(e + 1) * 128] = ml_dtypes.bfloat16(1.0)
    masku = np.ascontiguousarray(np.triu(np.ones((128, 128), dtype=np.float32)))

    nc = _get_nc()
    in_maps = []
    for c in range(N_CORES):
        pm = np.zeros((K, NE), dtype=np.float32)
        pm[:, :c] = 1.0
        in_maps.append(dict(
            x=np.ascontiguousarray(x[c * T:(c + 1) * T]),
            wqt=wqt, wkt=wkt, wvt=wvt, wot=wot,
            rw=rw, rbbc=rbbc, w1=w1, b1r=b1r, w2=w2, b2=b2,
            ident=ident, gsel=gsel, masku=masku, pmask=pm,
        ))
    res = run_bass_kernel_spmd(nc, in_maps, core_ids=list(range(N_CORES)))
    if DEBUG:
        kernel._last = res
    out = np.concatenate([res.results[c]["out"] for c in range(N_CORES)], axis=0)
    return out.reshape(1, 16384, 1024)
